# version 43
# baseline (speedup 1.0000x reference)
import contextlib
import math
import numpy as np
import concourse.bass as bass
import concourse.mybir as mybir
from concourse.bass_utils import run_bass_kernel_spmd

F32 = mybir.dt.float32
BF16 = mybir.dt.bfloat16
AF = mybir.ActivationFunctionType
ALU = mybir.AluOpType
AX = mybir.AxisListType

D = 2048
SEQ = 2048
NB = 4
NH = 16
RT_F = 16
RT_O = 8
NQ = 1024
SCALE = 128.0 ** -0.5
EPS_LN = 1e-5
EPS_RMS = 1e-6

ENG_SEM = {"pe": "spe", "dve": "sdv", "act": "sac", "pool": "spl", "sp": "ssp"}
COMPUTE_SEMS = ("spe", "sdv", "sac", "spl")
DMA_POOLS = {"dx": [f"dx{i}" for i in range(6)],
             "dwp": [f"dwp{i}" for i in range(6)],
             "dso": [f"dso{i}" for i in range(4)]}
ALL_DMA_SEMS = [s for pool in DMA_POOLS.values() for s in pool]


def _yarn_cos_sin(seq_len=SEQ, dim=64):
    ROPE_BASE, FACTOR = 10000.0, 40.0
    BETA_FAST, BETA_SLOW = 32.0, 1.0
    ORIG_MAX, MSCALE = 4096, 0.707

    def mscale(scale, m=1.0):
        if scale <= 1:
            return 1.0
        return 0.1 * m * math.log(scale) + 1.0

    exponents = np.arange(0, dim, 2, dtype=np.float64) / dim
    freq_extra = 1.0 / ROPE_BASE ** exponents
    freq_inter = 1.0 / (FACTOR * ROPE_BASE ** exponents)

    def corr_dim(num_rot):
        return dim * math.log(ORIG_MAX / (num_rot * 2 * math.pi)) / (2 * math.log(ROPE_BASE))

    low = max(math.floor(corr_dim(BETA_FAST)), 0)
    high = min(math.ceil(corr_dim(BETA_SLOW)), dim - 1)
    if low == high:
        high = high + 0.001
    ramp = np.clip((np.arange(dim // 2, dtype=np.float64) - low) / (high - low), 0.0, 1.0)
    inv_freq = freq_inter * ramp + freq_extra * (1.0 - ramp)
    t = np.arange(seq_len, dtype=np.float64)
    freqs = np.outer(t, inv_freq)
    msc = mscale(FACTOR, MSCALE) / mscale(FACTOR, 0.0)
    emb = np.concatenate([freqs, freqs], axis=-1)
    cos = (np.cos(emb) * msc).astype(np.float32).T
    sin = (np.sin(emb) * msc).astype(np.float32).T
    sin_signed = sin.copy()
    sin_signed[0:32] *= -1.0
    return np.ascontiguousarray(cos), np.ascontiguousarray(sin_signed)


_PERM = np.arange(64).reshape(32, 2).T.reshape(64).copy()
_PERM_ROT = _PERM[(np.arange(64) + 32) % 64].copy()


class Buf:
    __slots__ = ("w", "r", "name")

    def __init__(self, name=""):
        self.name = name
        self.w = {}
        self.r = {}


def _mx(d, s, v):
    if d.get(s, 0) < v:
        d[s] = v


class Prog:
    def __init__(self, nc, sems):
        self.nc = nc
        self.hs = sems
        self.cnt = {k: 0 for k in sems}
        self.sat = {}
        self.eng = {"pe": nc.tensor, "dve": nc.vector, "act": nc.scalar,
                    "sp": nc.sync, "pool": nc.gpsimd}
        self.bufs = {}
        self.dma_rr = {k: 0 for k in DMA_POOLS}

    def B(self, name):
        if name not in self.bufs:
            self.bufs[name] = Buf(name)
        return self.bufs[name]

    def run(self, e, reads, writes, emitfn, dma=None):
        need = {}
        for b in reads:
            for s, v in b.w.items():
                _mx(need, s, v)
        for b in writes:
            for s, v in b.w.items():
                _mx(need, s, v)
            for s, v in b.r.items():
                _mx(need, s, v)
        if e == "pe":
            need.pop("spe", None)
        if dma:
            pool = DMA_POOLS[dma]
            sem = pool[self.dma_rr[dma] % len(pool)]
            self.dma_rr[dma] += 1
            _mx(need, sem, self.cnt[sem])
        else:
            sem = ENG_SEM[e]
        sat = self.sat.setdefault(e, {})
        eng = self.eng[e]
        for s in sorted(need):
            v = need[s]
            if v > sat.get(s, 0):
                eng.wait_ge(self.hs[s], v)
                sat[s] = v
        inst = emitfn(eng)
        step = 16 if dma else 1
        inst.then_inc(self.hs[sem], step)
        self.cnt[sem] += step
        tok = (sem, self.cnt[sem])
        for b in reads:
            _mx(b.r, *tok)
        for b in writes:
            _mx(b.w, *tok)
        return tok

    def raw_wait(self, e, toks):
        sat = self.sat.setdefault(e, {})
        eng = self.eng[e]
        for s, v in toks:
            if v > 0 and v > sat.get(s, 0):
                eng.wait_ge(self.hs[s], v)
                sat[s] = v

    def phase_barrier(self):
        toks = [(s, self.cnt[s]) for s in COMPUTE_SEMS]
        toks += [(s, self.cnt[s]) for s in ALL_DMA_SEMS]
        for e in ("pe", "dve", "act", "pool", "sp"):
            self.raw_wait(e, toks)


def build_full(debug=False):
    nc = bass.Bass("TRN2", target_bir_lowering=False)

    def din(name, shape):
        return nc.dram_tensor(name, shape, F32, kind="ExternalInput").ap()

    x_full = din("x_full", [SEQ, D])
    wqkva_d = din("wqkva", [D, 192])
    wqbaug_d = din("wqbaug", [64, NH * 192])
    qnw_d = din("qnw", [64, 1])
    kvnw_d = din("kvnw", [64, 1])
    ktwt_d = din("ktwt", [128, NH * 128])
    wkvbv_d = din("wkvbv", [64, NH * 64])
    wo_d = din("wo", [NH * 64, D])
    wt_d = din("wt", [D, D])
    agw_d = din("agw", [D, 136])
    bcat_d = din("bcat", [128, D])
    brow_d = din("brow", [1, D])
    gateb_d = din("gateb", [128, 8])
    cosk_d = din("cosk", [64, SEQ])
    sink_d = din("sink", [64, SEQ])
    ident_d = din("ident", [128, 128])
    ones_d = din("ones1", [1, 128])
    sel_d = din("sel8", [8, 512])

    out_d = nc.dram_tensor("out", [NQ, D], F32, kind="ExternalOutput").ap()
    xmid_d = nc.dram_tensor("xmid", [NQ, D], F32,
                            kind="ExternalOutput" if debug else "Internal").ap()
    dbg = {}
    if debug:
        for nm, shp in [("d_cT", [128, SEQ]), ("d_rqT", [64, NQ]),
                        ("d_V0", [128, 1040]), ("d_KT15", [128, SEQ]),
                        ("d_QT15", [128, NQ]), ("d_woL0", [128, NQ]),
                        ("d_woLall", [128, 8 * NQ]), ("d_recB", [64, NQ]),
                        ("d_xaw", [128, NQ])]:
            dbg[nm] = nc.dram_tensor(nm, shp, F32, kind="ExternalOutput").ap()

    with contextlib.ExitStack() as ST:
        E = ST.enter_context
        sems = {}
        for s in ("spe", "sdv", "sac", "spl", "ssp", *ALL_DMA_SEMS):
            sems[s] = E(nc.semaphore(s))
        P = Prog(nc, sems)
        B = P.B

        ident_b = E(nc.sbuf_tensor("ident_b", [128, 128], BF16))
        ones_b = E(nc.sbuf_tensor("ones_b", [1, 128], BF16))
        sel_b = E(nc.sbuf_tensor("sel_b", [8, 512], BF16))
        brow_b = E(nc.sbuf_tensor("brow_b", [1, D], BF16))
        gateb_b = E(nc.sbuf_tensor("gateb_b", [128, 8], F32))
        qnw_b = E(nc.sbuf_tensor("qnw_b", [64, 1], F32))
        kvnw_b = E(nc.sbuf_tensor("kvnw_b", [64, 1], F32))
        eps_b = E(nc.sbuf_tensor("eps_b", [128, 1], F32))
        eps_r = E(nc.sbuf_tensor("eps_r", [128, 1], F32))
        st6b = [E(nc.sbuf_tensor(f"st6b{i}", [128, 24], F32)) for i in range(2)]
        mvb = [E(nc.sbuf_tensor(f"mvb{i}", [128, 2], F32)) for i in range(2)]
        rstdb = [E(nc.sbuf_tensor(f"rstdb{i}", [128, 1], F32)) for i in range(2)]
        nbb = [E(nc.sbuf_tensor(f"nbb{i}", [128, 1], F32)) for i in range(2)]

        blk = E(nc.Block())

        @blk.gpsimd
        def _section(_g):
            ST_WA = contextlib.ExitStack()
            Wt_sb = [ST_WA.enter_context(nc.sbuf_tensor(f"Wt{t}", [128, D], BF16))
                     for t in range(8)]
            ST_wo = contextlib.ExitStack()
            wo_b = ST_wo.enter_context(nc.sbuf_tensor("wo_b", [128, 8 * D], BF16))
            woL = [ST_wo.enter_context(nc.sbuf_tensor(f"woL{p}", [128, NQ], BF16))
                   for p in range(8)]
            ST12 = contextlib.ExitStack()
            E12 = ST12.enter_context
            cT = E12(nc.sbuf_tensor("cT", [128, SEQ], BF16))
            rqT = E12(nc.sbuf_tensor("rqT", [64, NQ], BF16))
            Vsb = [E12(nc.sbuf_tensor(f"V{rt}", [128, 1040], BF16))
                   for rt in range(RT_F)]
            cosk_b = E12(nc.sbuf_tensor("cosk_b", [64, SEQ], BF16))
            sink_b = E12(nc.sbuf_tensor("sink_b", [64, SEQ], BF16))
            ktwt_b = E12(nc.sbuf_tensor("ktwt_b", [128, NH * 128], BF16))
            wqbaug_b = E12(nc.sbuf_tensor("wqbaug_b", [64, NH * 192], BF16))

            ST1 = contextlib.ExitStack()
            E1 = ST1.enter_context
            wqkva_b = E1(nc.sbuf_tensor("wqkva_b", [128, RT_F * 192], BF16))
            wkvbv_b = E1(nc.sbuf_tensor("wkvbv_b", [64, NH * 64], BF16))
            xtmp = [E1(nc.sbuf_tensor(f"xtmp{i}", [128, D], F32)) for i in range(3)]
            lnb = [E1(nc.sbuf_tensor(f"lnb{i}", [128, D], BF16)) for i in range(2)]
            slab = [E1(nc.sbuf_tensor(f"slab{i}", [128, D], BF16)) for i in range(2)]
            prj = [E1(nc.sbuf_tensor(f"prj{i}", [128, 192], BF16)) for i in range(2)]
            sqs = [E1(nc.sbuf_tensor(f"sqs{i}", [128, 64], F32)) for i in range(2)]
            st6 = [E1(nc.sbuf_tensor(f"st6{i}", [128, 24], F32)) for i in range(2)]
            mv = [E1(nc.sbuf_tensor(f"mv{i}", [128, 2], F32)) for i in range(2)]
            rstd = [E1(nc.sbuf_tensor(f"rstd{i}", [128, 1], F32)) for i in range(6)]
            nbias = [E1(nc.sbuf_tensor(f"nbias{i}", [128, 1], F32)) for i in range(2)]
            kpF = E1(nc.sbuf_tensor("kpF", [64, SEQ], BF16))
            krx = E1(nc.sbuf_tensor("krx", [64, SEQ], BF16))

            psP = E1(nc.psum_tensor("psP", [128, 1024], F32))
            psC = E1(nc.psum_tensor("psC", [128, 512], F32))
            psQ = E1(nc.psum_tensor("psQ", [128, 512], F32))
            psV = [E1(nc.psum_tensor(f"psV{i}", [128, 1024], F32))
                   for i in range(2)]

            for name, dst, srcap in [
                ("qnw", qnw_b, qnw_d), ("kvnw", kvnw_b, kvnw_d),
                ("gateb", gateb_b, gateb_d),
            ]:
                P.run("sp", [], [B(name)],
                      lambda e, dst=dst, srcap=srcap: e.dma_start(out=dst[:, :],
                                                                  in_=srcap[:, :]),
                      dma="dx")
            for name, dst, srcap in [
                ("cosk", cosk_b, cosk_d), ("sink", sink_b, sink_d),
                ("wqbaug", wqbaug_b, wqbaug_d), ("ktwt", ktwt_b, ktwt_d),
                ("wkvbv", wkvbv_b, wkvbv_d),
            ]:
                P.run("pool", [], [B(name)],
                      lambda e, dst=dst, srcap=srcap: e.dma_start(out=dst[:, :],
                                                                  in_=srcap[:, :]),
                      dma="dwp")
            P.run("pool", [], [B("eps")],
                  lambda e: e.memset(eps_b[:, :], EPS_LN))
            P.run("pool", [], [B("epsr")],
                  lambda e: e.memset(eps_r[:, :], EPS_RMS))
            P.run("pool", [], [B("ident")],
                  lambda e: e.dma_start(out=ident_b[:, :], in_=ident_d[:, :]),
                  dma="dwp")
            P.run("pool", [], [B("ones")],
                  lambda e: e.dma_start(out=ones_b[:, :], in_=ones_d[:, :]),
                  dma="dwp")
            P.run("pool", [], [B("sel")],
                  lambda e: e.dma_start(out=sel_b[:, :], in_=sel_d[:, :]),
                  dma="dwp")
            P.run("pool", [], [B("brow")],
                  lambda e: e.dma_start(out=brow_b[:, :], in_=brow_d[:, :]),
                  dma="dwp")
            P.run("pool", [], [B("wqkva")],
                  lambda e: e.dma_start(
                      out=wqkva_b[:, :].rearrange("p (t c) -> p t c", c=192),
                      in_=wqkva_d.rearrange("(t p) c -> p t c", p=128)),
                  dma="dwp")

            P.run("dve", [B("wqbaug"), B("qnw")], [B("wqbaug")],
                  lambda e: e.tensor_scalar(wqbaug_b[:, :], wqbaug_b[:, :],
                                            qnw_b[:, 0:1], None, ALU.mult))
            for rt in range(RT_F):
                P.run("pool", [], [B(f"V{rt}")],
                      lambda e, rt=rt: e.memset(Vsb[rt][:, :], 1.0))

            def emit_x_dma(rt):
                P.run("sp", [], [B(f"xtmp{rt % 3}")],
                      lambda e, rt=rt: e.dma_start(
                          out=xtmp[rt % 3][:, :],
                          in_=x_full[rt * 128:(rt + 1) * 128, :]),
                      dma="dx")

            emit_x_dma(0)
            emit_x_dma(1)
            emit_x_dma(2)

            def ln1_stats(rt):
                xb = xtmp[rt % 3]
                i2 = rt % 2
                xB = B(f"xtmp{rt % 3}")
                mvB, r0B = B(f"mv{i2}"), B(f"rstd{i2}")
                for i in range(4):
                    P.run("dve", [xB], [B(f"st6_{i2}_{i}")],
                          lambda e, i=i, i2=i2, xb=xb: e.bn_stats(
                              st6[i2][:, i * 6:(i + 1) * 6],
                              xb[:, i * 512:(i + 1) * 512]))
                P.run("dve", [B(f"st6_{i2}_{i}") for i in range(4)], [mvB],
                      lambda e, i2=i2: e.bn_aggr(mv[i2][:, :], st6[i2][:, :]))

            def ln1_apply(rt):
                xb = xtmp[rt % 3]
                i2 = rt % 2
                xB = B(f"xtmp{rt % 3}")
                lnB = B(f"lnb{i2}")
                mvB, r0B, nbB = B(f"mv{i2}"), B(f"rstd{i2}"), B(f"nbias{i2}")
                P.run("act", [mvB, B("eps"), r0B], [r0B],
                      lambda e, i2=i2: e.activation(rstd[i2][:, :], mv[i2][:, 1:2],
                                                    AF.Ln, bias=eps_b[:, 0:1]))
                P.run("act", [r0B], [r0B],
                      lambda e, i2=i2: e.activation(rstd[i2][:, :], rstd[i2][:, :],
                                                    AF.Exp, scale=-0.5))
                P.run("dve", [mvB, r0B], [nbB],
                      lambda e, i2=i2: e.scalar_tensor_tensor(
                          nbias[i2][:, :], mv[i2][:, 0:1], -1.0, rstd[i2][:, :],
                          ALU.mult, ALU.mult))
                P.run("act", [xB, r0B, nbB], [lnB],
                      lambda e, i2=i2, xb=xb: e.activation(
                          lnb[i2][:, :], xb[:, :], AF.Identity,
                          bias=nbias[i2][:, 0:1], scale=rstd[i2][:, 0:1]))

            ln1_stats(0)
            ln1_apply(0)

            def p1_front(rt):
                i2 = rt % 2
                lnB = B(f"lnb{i2}")
                if rt + 1 < RT_F:
                    ln1_stats(rt + 1)
                if rt + 3 < RT_F:
                    emit_x_dma(rt + 3)
                for bi in range(4):
                    half = bi % 2

                    def mmT(e, bi=bi, half=half):
                        for t in range(4):
                            tt = bi * 4 + t
                            inst = e.matmul(
                                psP[:, half * 512 + t * 128:half * 512 + (t + 1) * 128],
                                lnb[i2][:, tt * 128:(tt + 1) * 128],
                                ident_b[:, :], start=True, stop=True)
                        return inst
                    P.run("pe", [lnB, B("ident")], [B(f"psP{half}")], mmT)
                    if bi % 2 == 0:
                        P.run("act", [B(f"psP{half}")], [B(f"slab{i2}_{bi}")],
                              lambda e, bi=bi, half=half: e.activation(
                                  slab[i2][:, bi * 512:(bi + 1) * 512],
                                  psP[:, half * 512:(half + 1) * 512], AF.Identity))
                    else:
                        P.run("dve", [B(f"psP{half}")], [B(f"slab{i2}_{bi}")],
                              lambda e, bi=bi, half=half: e.tensor_copy(
                                  slab[i2][:, bi * 512:(bi + 1) * 512],
                                  psP[:, half * 512:(half + 1) * 512]))
                if rt + 1 < RT_F:
                    ln1_apply(rt + 1)

            def p1_back1(rt):
                i2 = rt % 2
                prB = B(f"prj{i2}")
                sqB, r1B, r2B = B(f"sqs{i2}"), B(f"rstd{2 + i2}"), B(f"rstd{4 + i2}")
                for g in range(4):
                    def mmC(e, g=g):
                        for t in range(g * 4, (g + 1) * 4):
                            inst = e.matmul(psC[:, 0:192],
                                            slab[i2][:, t * 128:(t + 1) * 128],
                                            wqkva_b[:, t * 192:(t + 1) * 192],
                                            start=(t == 0), stop=(t == RT_F - 1),
                                            skip_group_check=True)
                        return inst
                    P.run("pe", [B(f"slab{i2}_{g}"), B("wqkva")], [B("psC")], mmC)
                P.run("act", [B("psC")], [sqB, r1B],
                      lambda e: e.activation(sqs[i2][:, :], psC[:, 0:64], AF.Square,
                                             accum_out=rstd[2 + i2][:, 0:1]))
                P.run("act", [r1B, B("epsr")], [r1B],
                      lambda e: e.activation(rstd[2 + i2][:, :], rstd[2 + i2][:, :],
                                             AF.Ln, scale=1.0 / 64.0,
                                             bias=eps_r[:, 0:1]))
                P.run("act", [r1B], [r1B],
                      lambda e: e.activation(rstd[2 + i2][:, :], rstd[2 + i2][:, :],
                                             AF.Exp, scale=-0.5))
                P.run("act", [B("psC"), r1B], [prB],
                      lambda e: e.activation(prj[i2][:, 0:64], psC[:, 0:64],
                                             AF.Identity, scale=rstd[2 + i2][:, 0:1]))
                P.run("act", [B("psC")], [prB],
                      lambda e: e.activation(prj[i2][:, 64:128], psC[:, 64:128],
                                             AF.Identity))
                if rt < RT_O:
                    P.run("act", [B("psC")], [sqB, r2B],
                          lambda e: e.activation(sqs[i2][:, :], psC[:, 128:192],
                                                 AF.Square,
                                                 accum_out=rstd[4 + i2][:, 0:1]))
                    P.run("act", [r2B, B("epsr")], [r2B],
                          lambda e: e.activation(rstd[4 + i2][:, :],
                                                 rstd[4 + i2][:, :], AF.Ln,
                                                 scale=1.0 / 64.0,
                                                 bias=eps_r[:, 0:1]))
                    P.run("act", [r2B], [r2B],
                          lambda e: e.activation(rstd[4 + i2][:, :], rstd[4 + i2][:, :],
                                                 AF.Exp, scale=-0.5))
                    P.run("act", [B("psC"), r2B], [prB],
                          lambda e: e.activation(prj[i2][:, 128:192], psC[:, 128:192],
                                                 AF.Identity,
                                                 scale=rstd[4 + i2][:, 0:1]))

            def p1_back2(rt):
                i2 = rt % 2
                prB = B(f"prj{i2}")

                def mmP(e):
                    inst = e.matmul(psQ[:, 0:128], prj[i2][:, 0:128], ident_b[:, :],
                                    start=True, stop=True)
                    if rt < RT_O:
                        inst = e.matmul(psQ[0:64, 128:256], prj[i2][:, 128:192],
                                        ident_b[:, :], start=True, stop=True)
                    return inst
                P.run("pe", [prB, B("ident")], [B("psQ")], mmP)
                P.run("dve", [B("psQ"), B("kvnw")], [B("cT")],
                      lambda e: e.tensor_scalar(cT[0:64, rt * 128:(rt + 1) * 128],
                                                psQ[0:64, 0:128],
                                                kvnw_b[:, 0:1], None, ALU.mult))
                P.run("dve", [B("psQ")], [B("kpF")],
                      lambda e: e.tensor_copy(kpF[:, rt * 128:(rt + 1) * 128],
                                              psQ[64:128, 0:128]))
                if rt < RT_O:
                    P.run("dve", [B("psQ")], [B("rqT")],
                          lambda e: e.tensor_copy(rqT[:, rt * 128:(rt + 1) * 128],
                                                  psQ[0:64, 128:256]))

            def p1_back3(rt):
                pv = psV[rt % 2]

                def mmV(e):
                    e.matmul(pv[:, 0:512], cT[0:64, rt * 128:(rt + 1) * 128],
                             wkvbv_b[:, 0:512], start=True, stop=True)
                    return e.matmul(pv[:, 512:1024],
                                    cT[0:64, rt * 128:(rt + 1) * 128],
                                    wkvbv_b[:, 512:1024], start=True, stop=True)
                P.run("pe", [B("cT"), B("wkvbv")], [B(f"psV{rt % 2}")], mmV)
                P.run("act", [B(f"psV{rt % 2}")], [B(f"V{rt}")],
                      lambda e: e.activation(
                          Vsb[rt][:, :].rearrange("p (h c) -> p h c", c=65)[:, :, 0:64],
                          pv[:, 0:1024].rearrange("p (h c) -> p h c", c=64),
                          AF.Identity))

            for rt in range(RT_F):
                p1_front(rt)
                if rt >= 1:
                    p1_back1(rt - 1)
                if rt >= 2:
                    p1_back2(rt - 2)
                if rt >= 3:
                    p1_back3(rt - 3)
            p1_back1(RT_F - 1)
            p1_back2(RT_F - 2)
            p1_back3(RT_F - 3)
            p1_back2(RT_F - 1)
            p1_back3(RT_F - 2)
            p1_back3(RT_F - 1)

            P.run("dve", [B("kpF")], [B("krx")],
                  lambda e: e.tensor_copy(krx[0:32, :], kpF[32:64, :]))
            P.run("dve", [B("kpF")], [B("krx")],
                  lambda e: e.tensor_copy(krx[32:64, :], kpF[0:32, :]))
            P.run("dve", [B("krx"), B("sink")], [B("krx")],
                  lambda e: e.tensor_tensor(krx[:, :], krx[:, :], sink_b[:, :],
                                            ALU.mult))
            P.run("dve", [B("kpF"), B("cosk")], [B("kpF")],
                  lambda e: e.tensor_tensor(kpF[:, :], kpF[:, :], cosk_b[:, :],
                                            ALU.mult))
            P.run("dve", [B("kpF"), B("krx")], [B("cT")],
                  lambda e: e.tensor_tensor(cT[64:128, :], kpF[:, :], krx[:, :],
                                            ALU.add))

            P.phase_barrier()
            ST1.close()

            P.run("pool", [], [B("wo_b")],
                  lambda e: e.dma_start(
                      out=wo_b[:, :].rearrange("p (t c) -> p t c", c=D),
                      in_=wo_d.rearrange("(t p) c -> p t c", p=128)), dma="dwp")
            for t in range(8):
                P.run("pool", [], [B(f"Wt{t}")],
                      lambda e, t=t: e.dma_start(out=Wt_sb[t][:, :],
                                                 in_=wt_d[t * 128:(t + 1) * 128, :]),
                      dma="dwp")

            ST2 = contextlib.ExitStack()
            E2 = ST2.enter_context
            QX = [E2(nc.sbuf_tensor(f"QX{i}", [128, NQ], BF16)) for i in range(2)]
            QT = [E2(nc.sbuf_tensor(f"QT{i}", [128, NQ], BF16)) for i in range(2)]
            eS = [E2(nc.sbuf_tensor(f"eS{i}", [128, NQ], BF16)) for i in range(6)]
            den_r = E2(nc.sbuf_tensor("den_r", [1, NQ], BF16))
            recT_b = E2(nc.sbuf_tensor("recT_b", [128, 8], BF16))
            rec8_b = E2(nc.sbuf_tensor("rec8_b", [8, 128], BF16))
            recB = E2(nc.sbuf_tensor("recB", [64, NQ], BF16))
            u_s = E2(nc.sbuf_tensor("u_s", [64, 512], F32))
            t_s = E2(nc.sbuf_tensor("t_s", [64, 512], F32))

            psS = [E2(nc.psum_tensor(f"psS{i}", [128, 1024], F32)) for i in range(2)]
            psO = E2(nc.psum_tensor("psO", [128, 1024], F32))
            psPr = E2(nc.psum_tensor("psPr", [128, 1024], F32))

            def emit_head_prod_qx(h):
                hb = h % 2
                qxB = B(f"QX{hb}")
                for c in range(2):
                    half = c % 2
                    P.run("pe", [B("ktwt"), B(f"QT{hb}")], [B(f"psPr{half}")],
                          lambda e, c=c, half=half, h=h, hb=hb: e.matmul(
                              psPr[:, half * 512:(half + 1) * 512],
                              ktwt_b[:, h * 128:(h + 1) * 128],
                              QT[hb][:, c * 512:(c + 1) * 512],
                              start=True, stop=True))
                    P.run("dve", [B(f"psPr{half}")], [qxB],
                          lambda e, c=c, half=half, hb=hb: e.tensor_copy(
                              QX[hb][:, c * 512:(c + 1) * 512],
                              psPr[:, half * 512:(half + 1) * 512]))

            def emit_head_prod_q(h, c):
                hb = h % 2
                qtB = B(f"QT{hb}")

                def mmQ(e, c=c, h=h):
                    e.matmul(psPr[:, 0:512],
                             wqbaug_b[:, h * 192:h * 192 + 128],
                             rqT[:, c * 512:(c + 1) * 512], start=True, stop=True)
                    return e.matmul(psPr[0:64, 512:1024],
                                    wqbaug_b[:, h * 192 + 128:h * 192 + 192],
                                    rqT[:, c * 512:(c + 1) * 512],
                                    start=True, stop=True)
                P.run("pe", [B("wqbaug"), B("rqT")],
                      [B("psPr0"), B("psPr1")], mmQ)
                P.run("dve", [B("psPr0")], [qtB],
                      lambda e, c=c, hb=hb: e.tensor_copy(
                          QT[hb][0:64, c * 512:(c + 1) * 512],
                          psPr[0:64, 0:512]))
                P.run("dve", [B("psPr0"), B("cosk")], [B("u_s")],
                      lambda e, c=c: e.tensor_tensor(
                          u_s[:, :], psPr[64:128, 0:512],
                          cosk_b[:, c * 512:(c + 1) * 512], ALU.mult))
                P.run("dve", [B("psPr1"), B("sink")], [B("t_s")],
                      lambda e, c=c: e.tensor_tensor(
                          t_s[:, :], psPr[0:64, 512:1024],
                          sink_b[:, c * 512:(c + 1) * 512], ALU.mult))
                P.run("dve", [B("u_s"), B("t_s")], [qtB],
                      lambda e, c=c, hb=hb: e.tensor_tensor(
                          QT[hb][64:128, c * 512:(c + 1) * 512],
                          u_s[:, :], t_s[:, :], ALU.add))

            def emit_head_prod(h):
                emit_head_prod_q(h, 0)
                emit_head_prod_q(h, 1)
                emit_head_prod_qx(h)

            def head_den(h):
                def mmDT(e):
                    for c in range(8):
                        inst = e.matmul(psPr[:, c:c + 1],
                                        den_r[0:1, c * 128:(c + 1) * 128],
                                        ones_b[0:1, 0:1], start=True, stop=True)
                    return inst
                P.run("pe", [B("denr"), B("ones")], [B("psPr0")], mmDT)

                def _recip(e):
                    with nc.allow_low_precision(reason="1/denom stored bf16 for the broadcast matmul"):
                        return e.reciprocal(recT_b[:, :], psPr[:, 0:8])
                P.run("dve", [B("psPr0")], [B("recT")], _recip)

            def head_tail_a(h):
                def mm8(e):
                    return e.matmul(psPr[0:8, 128:256], recT_b[:, :],
                                    ident_b[:, :], start=True, stop=True)
                P.run("pe", [B("recT"), B("ident")], [B("psPr0")], mm8)
                P.run("dve", [B("psPr0")], [B("rec8")],
                      lambda e: e.tensor_copy(rec8_b[:, :], psPr[0:8, 128:256]))

            def head_tail_b(h):
                def mmBC(e):
                    for c in range(8):
                        inst = e.matmul(psPr[0:64, c * 128:(c + 1) * 128],
                                        sel_b[0:8, c * 64:(c + 1) * 64],
                                        rec8_b[0:8, 0:128], start=True, stop=True)
                    return inst
                P.run("pe", [B("sel"), B("rec8")], [B("psPr0"), B("psPr1")], mmBC)
                P.run("dve", [B("psPr0"), B("psPr1")], [B("recB")],
                      lambda e: e.tensor_copy(recB[:, :], psPr[0:64, 0:1024]))
                p, phalf = h // 2, h % 2
                P.run("dve", [B("psO"), B("recB")], [B(f"woL{p}")],
                      lambda e, p=p, phalf=phalf: e.tensor_tensor(
                          woL[p][phalf * 64:(phalf + 1) * 64, :],
                          psO[0:64, 0:1024], recB[:, :], ALU.mult))

            emit_head_prod(0)
            for h in range(NH):
                hb = h % 2
                qxB = B(f"QX{hb}")

                def emit_av(kt):
                    kb = kt % 6

                    def mmAV(e, kt=kt, kb=kb):
                        e.matmul(psO[0:65, 0:512], Vsb[kt][:, h * 65:(h + 1) * 65],
                                 eS[kb][:, 0:512], start=(kt == 0),
                                 stop=(kt == RT_F - 1), skip_group_check=True)
                        return e.matmul(psO[0:65, 512:1024],
                                        Vsb[kt][:, h * 65:(h + 1) * 65],
                                        eS[kb][:, 512:1024], start=(kt == 0),
                                        stop=(kt == RT_F - 1), skip_group_check=True)
                    P.run("pe", [B(f"V{kt}"), B(f"eS{kb}")], [B("psO")], mmAV)

                for kt in range(RT_F):
                    kb = kt % 6
                    ks = kt % 2

                    def mmS(e, kt=kt, ks=ks):
                        e.matmul(psS[ks][:, 0:512],
                                 cT[:, kt * 128:(kt + 1) * 128],
                                 QX[hb][:, 0:512], start=True, stop=True)
                        return e.matmul(psS[ks][:, 512:1024],
                                        cT[:, kt * 128:(kt + 1) * 128],
                                        QX[hb][:, 512:1024], start=True, stop=True)
                    P.run("pe", [B("cT"), qxB], [B(f"psS{ks}")], mmS)
                    P.run("act", [B(f"psS{ks}")], [B(f"eS{kb}")],
                          lambda e, kb=kb, ks=ks: e.activation(eS[kb][:, :],
                                                               psS[ks][:, 0:1024],
                                                               AF.Exp, scale=SCALE))
                    if h > 0:
                        if kt == 0:
                            head_den(h - 1)
                        elif kt == 1:
                            head_tail_a(h - 1)
                        elif kt == 2:
                            head_tail_b(h - 1)
                    if kt >= 5:
                        emit_av(kt - 5)
                    if h + 1 < NH:
                        if kt == 6:
                            emit_head_prod_q(h + 1, 0)
                        elif kt == 9:
                            emit_head_prod_q(h + 1, 1)
                        elif kt == 12:
                            emit_head_prod_qx(h + 1)
                emit_av(RT_F - 5)
                emit_av(RT_F - 4)
                emit_av(RT_F - 3)
                emit_av(RT_F - 2)
                emit_av(RT_F - 1)
                P.run("act", [B("psO")], [B("denr")],
                      lambda e: e.activation(den_r[0:1, :], psO[64:65, 0:1024],
                                             AF.Identity))
            head_den(NH - 1)
            head_tail_a(NH - 1)
            head_tail_b(NH - 1)

            if debug:
                for nm, tile, bb in [("d_cT", cT, B("cT")), ("d_KT15", QX[1], B("QX1")),
                                     ("d_QT15", QT[1], B("QT1")),
                                     ("d_woL0", woL[0], B("woL0"))]:
                    P.run("pool", [bb], [],
                          lambda e, nm=nm, tile=tile: e.dma_start(
                              out=dbg[nm][:, 0:tile.shape[1]], in_=tile[:, :]),
                          dma="dso")
                P.run("pool", [B("rqT")], [],
                      lambda e: e.dma_start(out=dbg["d_rqT"][:, :], in_=rqT[:, :]),
                      dma="dso")
                P.run("pool", [B("V0")], [],
                      lambda e: e.dma_start(out=dbg["d_V0"][:, :], in_=Vsb[0][:, :]),
                      dma="dso")
                for p in range(8):
                    P.run("pool", [B(f"woL{p}")], [],
                          lambda e, p=p: e.dma_start(
                              out=dbg["d_woLall"][:, p * NQ:(p + 1) * NQ],
                              in_=woL[p][:, :]), dma="dso")
                P.run("pool", [B("recB")], [],
                      lambda e: e.dma_start(out=dbg["d_recB"][:, :], in_=recB[:, :]),
                      dma="dso")

            P.phase_barrier()
            ST2.close()
            ST12.close()

            ST34 = contextlib.ExitStack()
            E4 = ST34.enter_context
            xre = [E4(nc.sbuf_tensor(f"xre{i}", [128, D], F32)) for i in range(2)]
            lnT2 = [E4(nc.sbuf_tensor(f"lnT2_{q}", [128, D], BF16))
                    for q in range(RT_O)]
            xaT = E4(nc.sbuf_tensor("xaT", [128, RT_O * 128], BF16))
            agw_b = E4(nc.sbuf_tensor("agw_b", [128, RT_F * 136], BF16))
            st6c = [E4(nc.sbuf_tensor(f"st6c{i}", [128, 24], F32)) for i in range(2)]
            mvc = [E4(nc.sbuf_tensor(f"mvc{i}", [128, 2], F32)) for i in range(2)]
            rstdc = [E4(nc.sbuf_tensor(f"rstdc{i}", [128, 1], F32)) for i in range(2)]
            nbc = [E4(nc.sbuf_tensor(f"nbc{i}", [128, 1], F32)) for i in range(2)]
            lg = E4(nc.sbuf_tensor("lg", [128, 8], F32))
            m1 = E4(nc.sbuf_tensor("m1", [128, 1], F32))
            m2 = E4(nc.sbuf_tensor("m2", [128, 1], F32))
            msk1 = E4(nc.sbuf_tensor("msk1", [128, 8], F32))
            msk2 = E4(nc.sbuf_tensor("msk2", [128, 8], F32))
            l2t = E4(nc.sbuf_tensor("l2t", [128, 8], F32))
            g1 = E4(nc.sbuf_tensor("g1", [128, 1], F32))
            g2 = E4(nc.sbuf_tensor("g2", [128, 1], F32))
            w8 = E4(nc.sbuf_tensor("w8", [128, 8], F32))
            e2s = E4(nc.sbuf_tensor("e2s", [128, 1], F32))
            xaw = [E4(nc.sbuf_tensor(f"xaw{i}", [128, 128], BF16))
                   for i in range(2)]
            w128 = E4(nc.sbuf_tensor("w128", [128, 128], F32))
            for t in range(8, RT_F):
                Wt_sb.append(E4(nc.sbuf_tensor(f"Wt{t}", [128, D], BF16)))
            ST34b = contextlib.ExitStack()
            ln2b = [ST34b.enter_context(nc.sbuf_tensor(f"ln2b{q}", [128, D], BF16))
                    for q in range(RT_O)]

            for t in range(8, RT_F):
                P.run("pool", [], [B(f"Wt{t}")],
                      lambda e, t=t: e.dma_start(out=Wt_sb[t][:, :],
                                                 in_=wt_d[t * 128:(t + 1) * 128, :]),
                      dma="dwp")
            P.run("pool", [], [B("agw")],
                  lambda e: e.dma_start(
                      out=agw_b[:, :].rearrange("p (t c) -> p t c", c=136),
                      in_=agw_d.rearrange("(t p) c -> p t c", p=128)), dma="dwp")

            ST3p = contextlib.ExitStack()
            psW = [ST3p.enter_context(nc.psum_tensor(f"psW{i}", [128, 2048], F32))
                   for i in range(2)]

            def emit_ln(qb, srcbuf, src_ap, dve_src_ap, outbuf, out_ap):
                for i in range(4):
                    P.run("dve", [srcbuf], [B(f"st6b{qb}_{i}")],
                          lambda e, i=i: e.bn_stats(st6b[qb][:, i * 6:(i + 1) * 6],
                                                    dve_src_ap(i)))
                P.run("dve", [B(f"st6b{qb}_{i}") for i in range(4)], [B(f"mvb{qb}")],
                      lambda e: e.bn_aggr(mvb[qb][:, :], st6b[qb][:, :]))
                P.run("act", [B(f"mvb{qb}"), B("eps")], [B(f"rstdb{qb}")],
                      lambda e: e.activation(rstdb[qb][:, :], mvb[qb][:, 1:2],
                                             AF.Ln, bias=eps_b[:, 0:1]))
                P.run("act", [B(f"rstdb{qb}")], [B(f"rstdb{qb}")],
                      lambda e: e.activation(rstdb[qb][:, :], rstdb[qb][:, :],
                                             AF.Exp, scale=-0.5))
                P.run("dve", [srcbuf, B(f"mvb{qb}"), B(f"rstdb{qb}")], [outbuf],
                      lambda e: e.tensor_scalar(out_ap, src_ap,
                                                mvb[qb][:, 0:1],
                                                rstdb[qb][:, 0:1],
                                                ALU.subtract, op1=ALU.mult))

            def wo_xre_dma(q):
                P.run("sp", [], [B(f"xre{q % 2}")],
                      lambda e, q=q: e.dma_start(
                          out=xre[q % 2][:, :], in_=x_full[q * 128:(q + 1) * 128, :]),
                      dma="dx")

            wo_xre_dma(0)
            for q in range(RT_O):
                qb = q % 2
                if q + 1 < RT_O:
                    wo_xre_dma(q + 1)

                for c in range(4):
                    def mmWO(e, q=q, qb=qb, c=c):
                        for p in range(8):
                            inst = e.matmul(
                                psW[qb][:, c * 512:(c + 1) * 512],
                                woL[p][:, q * 128:(q + 1) * 128],
                                wo_b[:, p * D + c * 512:p * D + (c + 1) * 512],
                                start=(p == 0), stop=(p == 7),
                                skip_group_check=True)
                        return inst
                    P.run("pe", [B("wo_b")] + [B(f"woL{p}") for p in range(8)],
                          [B(f"psW{qb}b{c}")], mmWO)
                    P.run("dve", [B(f"psW{qb}b{c}")], [B(f"st6b{qb}_{c}")],
                          lambda e, c=c, qb=qb: e.bn_stats(
                              st6b[qb][:, c * 6:(c + 1) * 6],
                              psW[qb][:, c * 512:(c + 1) * 512]))
                P.run("dve", [B(f"st6b{qb}_{i}") for i in range(4)], [B(f"mvb{qb}")],
                      lambda e, qb=qb: e.bn_aggr(mvb[qb][:, :], st6b[qb][:, :]))
                P.run("act", [B(f"mvb{qb}"), B("eps")], [B(f"rstdb{qb}")],
                      lambda e, qb=qb: e.activation(rstdb[qb][:, :], mvb[qb][:, 1:2],
                                                    AF.Ln, bias=eps_b[:, 0:1]))
                P.run("act", [B(f"rstdb{qb}")], [B(f"rstdb{qb}")],
                      lambda e, qb=qb: e.activation(rstdb[qb][:, :], rstdb[qb][:, :],
                                                    AF.Exp, scale=-0.5))
                for c in range(4):
                    P.run("dve", [B(f"psW{qb}b{c}"), B(f"mvb{qb}"),
                                  B(f"rstdb{qb}")], [B(f"psW{qb}b{c}")],
                          lambda e, c=c, qb=qb: e.tensor_scalar(
                              psW[qb][:, c * 512:(c + 1) * 512],
                              psW[qb][:, c * 512:(c + 1) * 512],
                              mvb[qb][:, 0:1], rstdb[qb][:, 0:1],
                              ALU.subtract, op1=ALU.mult))
                    P.run("dve", [B(f"xre{qb}"), B(f"psW{qb}b{c}")],
                          [B(f"xre{qb}")],
                          lambda e, c=c, qb=qb: e.tensor_tensor(
                              xre[qb][:, c * 512:(c + 1) * 512],
                              xre[qb][:, c * 512:(c + 1) * 512],
                              psW[qb][:, c * 512:(c + 1) * 512], ALU.add))
                if q >= RT_O - 2:
                    cadd = P.cnt["sdv"]
                P.run("pool", [B(f"xre{qb}")], [B(f"xmid{q}")],
                      lambda e, q=q, qb=qb: e.dma_start(
                          out=xmid_d[q * 128:(q + 1) * 128, :], in_=xre[qb][:, :]),
                      dma="dso")
                for i in range(4):
                    P.run("dve", [B(f"xre{qb}")], [B(f"st6c{qb}_{i}")],
                          lambda e, i=i, qb=qb: e.bn_stats(
                              st6c[qb][:, i * 6:(i + 1) * 6],
                              xre[qb][:, i * 512:(i + 1) * 512]))
                P.run("dve", [B(f"st6c{qb}_{i}") for i in range(4)], [B(f"mvc{qb}")],
                      lambda e, qb=qb: e.bn_aggr(mvc[qb][:, :], st6c[qb][:, :]))
                P.run("act", [B(f"mvc{qb}"), B("eps")], [B(f"rstdc{qb}")],
                      lambda e, qb=qb: e.activation(rstdc[qb][:, :], mvc[qb][:, 1:2],
                                                    AF.Ln, bias=eps_b[:, 0:1]))
                P.run("act", [B(f"rstdc{qb}")], [B(f"rstdc{qb}")],
                      lambda e, qb=qb: e.activation(rstdc[qb][:, :], rstdc[qb][:, :],
                                                    AF.Exp, scale=-0.5))
                P.run("dve", [B(f"mvc{qb}"), B(f"rstdc{qb}")], [B(f"nbc{qb}")],
                      lambda e, qb=qb: e.scalar_tensor_tensor(
                          nbc[qb][:, :], mvc[qb][:, 0:1], -1.0, rstdc[qb][:, :],
                          ALU.mult, ALU.mult))
                P.run("act", [B(f"xre{qb}"), B(f"rstdc{qb}"), B(f"nbc{qb}")],
                      [B(f"ln2b{q}")],
                      lambda e, q=q, qb=qb: e.activation(
                          ln2b[q][:, :], xre[qb][:, :], AF.Identity,
                          bias=nbc[qb][:, 0:1], scale=rstdc[qb][:, 0:1]))
                if q == RT_O - 2:
                    c6 = cadd

            c7 = cadd
            ST3p.close()

            ST4p = contextlib.ExitStack()
            psT2 = ST4p.enter_context(nc.psum_tensor("psT2", [128, 1024], F32))
            psT2b = ST4p.enter_context(nc.psum_tensor("psT2b", [128, 1024], F32))
            psA2 = [ST4p.enter_context(nc.psum_tensor(f"psA2_{i}", [128, 512], F32))
                    for i in range(2)]
            psXT = ST4p.enter_context(nc.psum_tensor("psXT", [128, 512], F32))
            P.raw_wait("pe", [("sdv", c6)])

            def mmT2_grp(q, bi):
                pst = psT2 if bi % 2 == 0 else psT2b
                half = (bi // 2) % 2

                def mm(e, q=q, bi=bi, pst=pst, half=half):
                    for t in range(4):
                        tt = bi * 4 + t
                        inst = e.matmul(
                            pst[:, half * 512 + t * 128:half * 512 + (t + 1) * 128],
                            ln2b[q][:, tt * 128:(tt + 1) * 128],
                            ident_b[:, :], start=True, stop=True)
                    return inst
                P.run("pe", [B(f"ln2b{q}"), B("ident")], [B(f"psT2{bi % 4}")], mm)

            def evac(q, bi, eng):
                pst = psT2 if bi % 2 == 0 else psT2b
                half = (bi // 2) % 2
                if eng == "act":
                    P.run("act", [B(f"psT2{bi % 4}")], [B(f"lnT2_{q}_{bi}")],
                          lambda e, q=q, bi=bi, pst=pst, half=half: e.activation(
                              lnT2[q][:, bi * 512:(bi + 1) * 512],
                              pst[:, half * 512:(half + 1) * 512], AF.Identity))
                else:
                    P.run("dve", [B(f"psT2{bi % 4}")], [B(f"lnT2_{q}_{bi}")],
                          lambda e, q=q, bi=bi, pst=pst, half=half: e.tensor_copy(
                              lnT2[q][:, bi * 512:(bi + 1) * 512],
                              pst[:, half * 512:(half + 1) * 512]))

            def mmA_grp(q, g):
                def mm(e, q=q, g=g):
                    for t in range(g * 4, (g + 1) * 4):
                        inst = e.matmul(psA2[q % 2][:, 0:136],
                                        lnT2[q][:, t * 128:(t + 1) * 128],
                                        agw_b[:, t * 136:(t + 1) * 136],
                                        start=(t == 0), stop=(t == RT_F - 1),
                                        skip_group_check=True)
                    return inst
                P.run("pe", [B(f"lnT2_{q}_{g}"), B("agw")],
                      [B(f"psA2_{q % 2}")], mm)

            def g_lg(q):
                P.run("dve", [B(f"psA2_{q % 2}"), B("gateb")], [B("lg")],
                      lambda e, q=q: e.tensor_tensor(lg[:, :],
                                                     psA2[q % 2][:, 128:136],
                                                     gateb_b[:, :], ALU.add))

            def g_m1(q):
                P.run("dve", [B("lg")], [B("m1")],
                      lambda e: e.reduce_max(m1[:, :], lg[:, :], axis=AX.X))

            def g_msk1(q):
                P.run("dve", [B("lg"), B("m1")], [B("msk1")],
                      lambda e: e.tensor_scalar(msk1[:, :], lg[:, :], m1[:, 0:1],
                                                None, ALU.is_ge))

            def g_l2t(q):
                P.run("dve", [B("msk1"), B("lg")], [B("l2t")],
                      lambda e: e.scalar_tensor_tensor(l2t[:, :], msk1[:, :], -1e9,
                                                       lg[:, :], ALU.mult, ALU.add))

            def g_m2(q):
                P.run("dve", [B("l2t")], [B("m2")],
                      lambda e: e.reduce_max(m2[:, :], l2t[:, :], axis=AX.X))

            def g_msk2(q):
                P.run("dve", [B("l2t"), B("m2")], [B("msk2")],
                      lambda e: e.tensor_scalar(msk2[:, :], l2t[:, :], m2[:, 0:1],
                                                None, ALU.is_ge))

            def g_e2s(q):
                P.run("dve", [B("m2"), B("m1")], [B("e2s")],
                      lambda e: e.tensor_tensor(e2s[:, :], m2[:, :], m1[:, :],
                                                ALU.subtract))
                P.run("act", [B("e2s")], [B("e2s")],
                      lambda e: e.activation(e2s[:, :], e2s[:, :], AF.Exp))

            def g_tail(q):
                P.run("dve", [B("e2s")], [B("g1")],
                      lambda e: e.tensor_scalar(g1[:, :], e2s[:, :], 1.0, None,
                                                ALU.add))
                P.run("dve", [B("g1")], [B("g1")],
                      lambda e: e.reciprocal(g1[:, :], g1[:, :]))
                P.run("dve", [B("e2s"), B("g1")], [B("g2")],
                      lambda e: e.tensor_tensor(g2[:, :], e2s[:, :], g1[:, :],
                                                ALU.mult))
                P.run("dve", [B("msk1"), B("g1")], [B("w8")],
                      lambda e: e.tensor_scalar(w8[:, :], msk1[:, :], g1[:, 0:1],
                                                None, ALU.mult))
                P.run("dve", [B("msk2"), B("g2"), B("w8")], [B("w8")],
                      lambda e: e.scalar_tensor_tensor(w8[:, :], msk2[:, :],
                                                       g2[:, 0:1], w8[:, :],
                                                       ALU.mult, ALU.add))
                P.run("dve", [B("w8")], [B("w128")],
                      lambda e: e.tensor_copy(
                          w128[:, :].rearrange("p (a b) -> p a b", a=8),
                          bass.AP(w8, 0, [[8, 128], [1, 8], [0, 16]])))
                P.run("dve", [B(f"psA2_{q % 2}"), B("w128")], [B(f"xaw{q % 2}")],
                      lambda e, q=q: e.tensor_tensor(
                          xaw[q % 2][:, :], psA2[q % 2][:, 0:128], w128[:, :],
                          ALU.mult))

            def backT(q):
                P.run("pe", [B(f"xaw{q % 2}"), B("ident")], [B("psXT")],
                      lambda e, q=q: e.matmul(psXT[:, 0:128], xaw[q % 2][:, :],
                                              ident_b[:, :], start=True, stop=True))

            def backTevac(q):
                P.run("dve", [B("psXT")], [B("xaT")],
                      lambda e, q=q: e.tensor_copy(xaT[:, q * 128:(q + 1) * 128],
                                                   psXT[:, 0:128]))

            for q in range(RT_O):
                if q == 1:
                    P.raw_wait("pe", [("sdv", c7)])
                if q >= 1:
                    for g in range(4):
                        mmA_grp(q - 1, g)
                mmT2_grp(q, 0)
                evac(q, 0, "act")
                if q >= 1:
                    g_lg(q - 1)
                mmT2_grp(q, 1)
                if q >= 2:
                    backT(q - 2)
                evac(q, 1, "dve")
                if q >= 1:
                    g_m1(q - 1)
                mmT2_grp(q, 2)
                evac(q, 2, "act")
                if q >= 2:
                    backTevac(q - 2)
                if q >= 1:
                    g_msk1(q - 1)
                    g_l2t(q - 1)
                mmT2_grp(q, 3)
                evac(q, 3, "dve")
                if q >= 1:
                    g_m2(q - 1)
                    g_msk2(q - 1)
                    g_e2s(q - 1)
                    g_tail(q - 1)
            for g in range(4):
                mmA_grp(RT_O - 1, g)
            g_lg(RT_O - 1)
            g_m1(RT_O - 1)
            backT(RT_O - 2)
            backTevac(RT_O - 2)
            g_msk1(RT_O - 1)
            g_l2t(RT_O - 1)
            g_m2(RT_O - 1)
            g_msk2(RT_O - 1)
            g_e2s(RT_O - 1)
            g_tail(RT_O - 1)
            backT(RT_O - 1)
            backTevac(RT_O - 1)
            if debug:
                P.run("pool", [B("xaT")], [],
                      lambda e: e.dma_start(out=dbg["d_xaw"][:, 0:1024],
                                            in_=xaT[:, 0:1024]), dma="dso")

            cp2 = (P.cnt["spe"], P.cnt["sdv"], P.cnt["sac"])
            ST4p.close()
            ST34b.close()

            ST34c = contextlib.ExitStack()
            bcat_b = ST34c.enter_context(nc.sbuf_tensor("bcat_b", [128, D], BF16))
            P.raw_wait("pool", [("spe", cp2[0])])
            P.run("pool", [], [B("bcat")],
                  lambda e: e.dma_start(out=bcat_b[:, :], in_=bcat_d[:, :]),
                  dma="dwp")
            ST5 = contextlib.ExitStack()
            psF = [ST5.enter_context(nc.psum_tensor(f"psF{i}", [128, 2048], F32))
                   for i in range(2)]
            P.raw_wait("pe", [("sdv", cp2[1]), ("sac", cp2[2])])

            def w_xre_dma(q):
                P.run("sp", [B(f"xmid{q}")], [B(f"xre{q % 2}")],
                      lambda e, q=q: e.dma_start(
                          out=xre[q % 2][:, :], in_=xmid_d[q * 128:(q + 1) * 128, :]),
                      dma="dx")

            w_xre_dma(0)
            for q in range(RT_O):
                qb = q % 2
                if q + 1 < RT_O:
                    w_xre_dma(q + 1)
                for t in range(RT_F):
                    def mmW(e, t=t, q=q, qb=qb):
                        for c in range(4):
                            inst = e.matmul(psF[qb][:, c * 512:(c + 1) * 512],
                                            lnT2[q][:, t * 128:(t + 1) * 128],
                                            Wt_sb[t][:, c * 512:(c + 1) * 512],
                                            start=(t == 0), stop=False,
                                            skip_group_check=True)
                        return inst
                    P.run("pe", [B(f"lnT2_{q}_{t // 4}"), B(f"Wt{t}")],
                          [B(f"psF{qb}")], mmW)

                def mmLB(e, q=q, qb=qb):
                    for c in range(4):
                        e.matmul(psF[qb][:, c * 512:(c + 1) * 512],
                                 xaT[:, q * 128:(q + 1) * 128],
                                 bcat_b[:, c * 512:(c + 1) * 512],
                                 start=False, stop=False, skip_group_check=True)
                    for c in range(4):
                        inst = e.matmul(psF[qb][:, c * 512:(c + 1) * 512],
                                        ones_b[0:1, :],
                                        brow_b[0:1, c * 512:(c + 1) * 512],
                                        start=False, stop=True,
                                        skip_group_check=True)
                    return inst
                P.run("pe", [B("xaT"), B("bcat"), B("ones"), B("brow")],
                      [B(f"psF{qb}")], mmLB)
                emit_ln(qb, B(f"psF{qb}"), psF[qb][:, :],
                        lambda i, qb=qb: psF[qb][:, i * 512:(i + 1) * 512],
                        B(f"psF{qb}"), psF[qb][:, :])
                P.run("dve", [B(f"xre{qb}"), B(f"psF{qb}")], [B(f"xre{qb}")],
                      lambda e, qb=qb: e.tensor_tensor(xre[qb][:, :], xre[qb][:, :],
                                                       psF[qb][:, :], ALU.add))
                P.run("pool", [B(f"xre{qb}")], [],
                      lambda e, q=q, qb=qb: e.dma_start(
                          out=out_d[q * 128:(q + 1) * 128, :], in_=xre[qb][:, :]),
                      dma="dso")

            P.raw_wait("pool", [(s, P.cnt[s]) for s in DMA_POOLS["dso"]])
            ST5.close()
            ST34c.close()
            ST34.close()
            ST_wo.close()
            ST_WA.close()

    return nc



def make_in_maps(ins):
    hs = np.ascontiguousarray(np.asarray(ins["L0_attn_hidden_states"], dtype=np.float32))
    wq_a = np.asarray(ins["L0_attn_wq_a"], dtype=np.float32)
    qnw = np.asarray(ins["L0_attn_q_a_norm_w"], dtype=np.float32)
    wq_b = np.asarray(ins["L0_attn_wq_b"], dtype=np.float32)
    wkv_a = np.asarray(ins["L0_attn_wkv_a"], dtype=np.float32)
    kvnw = np.asarray(ins["L0_attn_kv_a_norm_w"], dtype=np.float32)
    wkv_b = np.asarray(ins["L0_attn_wkv_b"], dtype=np.float32)
    wo = np.asarray(ins["L0_attn_wo"], dtype=np.float32)
    W = np.asarray(ins["L0_ffn_W"], dtype=np.float32)
    bffn = np.asarray(ins["L0_ffn_b"], dtype=np.float32)
    lora_A = np.asarray(ins["L0_ffn_lora_A"], dtype=np.float32)
    lora_B = np.asarray(ins["L0_ffn_lora_B"], dtype=np.float32)
    gate_w = np.asarray(ins["L0_ffn_gate_w"], dtype=np.float32)
    gate_b = np.asarray(ins["L0_ffn_gate_b"], dtype=np.float32)

    cosk, sink = _yarn_cos_sin()

    wqkva = np.concatenate([wkv_a[:, 0:64], wkv_a[:, 64 + _PERM], wq_a], axis=1)

    wqbaug = np.empty((64, NH * 192), np.float32)
    for h in range(NH):
        base = h * 128
        wqbaug[:, h * 192:h * 192 + 64] = wq_b[:, base:base + 64]
        wqbaug[:, h * 192 + 64:h * 192 + 128] = wq_b[:, base + 64 + _PERM]
        wqbaug[:, h * 192 + 128:h * 192 + 192] = wq_b[:, base + 64 + _PERM_ROT]

    ktwt = np.zeros((128, NH * 128), np.float32)
    eye64 = np.eye(64, dtype=np.float32)
    for h in range(NH):
        ktwt[0:64, h * 128:h * 128 + 64] = wkv_b[:, h * 128:h * 128 + 64].T
        ktwt[64:128, h * 128 + 64:h * 128 + 128] = eye64

    wkvbv = np.empty((64, NH * 64), np.float32)
    for h in range(NH):
        wkvbv[:, h * 64:(h + 1) * 64] = wkv_b[:, h * 128 + 64:h * 128 + 128]

    agw = np.concatenate([lora_A.transpose(2, 0, 1).reshape(D, 128), gate_w.T],
                         axis=1)
    bcat = np.ascontiguousarray(lora_B.transpose(0, 2, 1).reshape(128, D))
    wt = np.ascontiguousarray(W.T)
    shared = dict(
        wqkva=wqkva, wqbaug=wqbaug,
        qnw=np.ascontiguousarray(qnw.reshape(64, 1)),
        kvnw=np.ascontiguousarray(kvnw.reshape(64, 1)),
        ktwt=ktwt, wkvbv=wkvbv, wo=wo, wt=wt, agw=agw, bcat=bcat,
        brow=bffn.reshape(1, D),
        gateb=np.broadcast_to(gate_b.reshape(1, 8), (128, 8)),
        ident=np.eye(128, dtype=np.float32),
        ones1=np.ones((1, 128), np.float32),
        sel8=np.repeat(np.eye(8, dtype=np.float32), 64, axis=1))
    shared = {k: np.ascontiguousarray(v, dtype=np.float32) for k, v in shared.items()}

    in_maps = []
    for c in range(8):
        b, half = c // 2, c % 2
        order = np.arange(SEQ) if half == 0 else np.r_[NQ:SEQ, 0:NQ]
        m = dict(shared)
        m["x_full"] = np.ascontiguousarray(hs[b][order])
        m["cosk"] = np.ascontiguousarray(cosk[:, order])
        m["sink"] = np.ascontiguousarray(sink[:, order])
        in_maps.append(m)
    return in_maps


def kernel(**ins):
    in_maps = make_in_maps(ins)
    nc = build_full(debug=False)
    res = run_bass_kernel_spmd(nc, in_maps, core_ids=list(range(8)))
    out = np.empty((NB, SEQ, D), np.float32)
    for c in range(8):
        b, half = c // 2, c % 2
        out[b, half * NQ:(half + 1) * NQ, :] = np.asarray(res.results[c]["out"])
    return out


# version 44 (final)
# speedup vs baseline: 1.0268x; 1.0268x over previous
import contextlib
import math
import numpy as np
import concourse.bass as bass
import concourse.mybir as mybir
from concourse.bass_utils import run_bass_kernel_spmd

F32 = mybir.dt.float32
BF16 = mybir.dt.bfloat16
AF = mybir.ActivationFunctionType
ALU = mybir.AluOpType
AX = mybir.AxisListType

D = 2048
SEQ = 2048
NB = 4
NH = 16
RT_F = 16
RT_O = 8
NQ = 1024
SCALE = 128.0 ** -0.5
EPS_LN = 1e-5
EPS_RMS = 1e-6

ENG_SEM = {"pe": "spe", "dve": "sdv", "act": "sac", "pool": "spl", "sp": "ssp"}
COMPUTE_SEMS = ("spe", "sdv", "sac", "spl")
DMA_POOLS = {"dx": [f"dx{i}" for i in range(6)],
             "dwp": [f"dwp{i}" for i in range(6)],
             "dso": [f"dso{i}" for i in range(4)]}
ALL_DMA_SEMS = [s for pool in DMA_POOLS.values() for s in pool]


def _yarn_cos_sin(seq_len=SEQ, dim=64):
    ROPE_BASE, FACTOR = 10000.0, 40.0
    BETA_FAST, BETA_SLOW = 32.0, 1.0
    ORIG_MAX, MSCALE = 4096, 0.707

    def mscale(scale, m=1.0):
        if scale <= 1:
            return 1.0
        return 0.1 * m * math.log(scale) + 1.0

    exponents = np.arange(0, dim, 2, dtype=np.float64) / dim
    freq_extra = 1.0 / ROPE_BASE ** exponents
    freq_inter = 1.0 / (FACTOR * ROPE_BASE ** exponents)

    def corr_dim(num_rot):
        return dim * math.log(ORIG_MAX / (num_rot * 2 * math.pi)) / (2 * math.log(ROPE_BASE))

    low = max(math.floor(corr_dim(BETA_FAST)), 0)
    high = min(math.ceil(corr_dim(BETA_SLOW)), dim - 1)
    if low == high:
        high = high + 0.001
    ramp = np.clip((np.arange(dim // 2, dtype=np.float64) - low) / (high - low), 0.0, 1.0)
    inv_freq = freq_inter * ramp + freq_extra * (1.0 - ramp)
    t = np.arange(seq_len, dtype=np.float64)
    freqs = np.outer(t, inv_freq)
    msc = mscale(FACTOR, MSCALE) / mscale(FACTOR, 0.0)
    emb = np.concatenate([freqs, freqs], axis=-1)
    cos = (np.cos(emb) * msc).astype(np.float32).T
    sin = (np.sin(emb) * msc).astype(np.float32).T
    sin_signed = sin.copy()
    sin_signed[0:32] *= -1.0
    return np.ascontiguousarray(cos), np.ascontiguousarray(sin_signed)


_PERM = np.arange(64).reshape(32, 2).T.reshape(64).copy()
_PERM_ROT = _PERM[(np.arange(64) + 32) % 64].copy()


class Buf:
    __slots__ = ("w", "r", "name")

    def __init__(self, name=""):
        self.name = name
        self.w = {}
        self.r = {}


def _mx(d, s, v):
    if d.get(s, 0) < v:
        d[s] = v


class Prog:
    def __init__(self, nc, sems):
        self.nc = nc
        self.hs = sems
        self.cnt = {k: 0 for k in sems}
        self.sat = {}
        self.eng = {"pe": nc.tensor, "dve": nc.vector, "act": nc.scalar,
                    "sp": nc.sync, "pool": nc.gpsimd}
        self.bufs = {}
        self.dma_rr = {k: 0 for k in DMA_POOLS}

    def B(self, name):
        if name not in self.bufs:
            self.bufs[name] = Buf(name)
        return self.bufs[name]

    def run(self, e, reads, writes, emitfn, dma=None):
        need = {}
        for b in reads:
            for s, v in b.w.items():
                _mx(need, s, v)
        for b in writes:
            for s, v in b.w.items():
                _mx(need, s, v)
            for s, v in b.r.items():
                _mx(need, s, v)
        if e == "pe":
            need.pop("spe", None)
        if dma:
            pool = DMA_POOLS[dma]
            sem = pool[self.dma_rr[dma] % len(pool)]
            self.dma_rr[dma] += 1
            _mx(need, sem, self.cnt[sem])
        else:
            sem = ENG_SEM[e]
        sat = self.sat.setdefault(e, {})
        eng = self.eng[e]
        for s in sorted(need):
            v = need[s]
            if v > sat.get(s, 0):
                eng.wait_ge(self.hs[s], v)
                sat[s] = v
        inst = emitfn(eng)
        step = 16 if dma else 1
        inst.then_inc(self.hs[sem], step)
        self.cnt[sem] += step
        tok = (sem, self.cnt[sem])
        for b in reads:
            _mx(b.r, *tok)
        for b in writes:
            _mx(b.w, *tok)
        return tok

    def raw_wait(self, e, toks):
        sat = self.sat.setdefault(e, {})
        eng = self.eng[e]
        for s, v in toks:
            if v > 0 and v > sat.get(s, 0):
                eng.wait_ge(self.hs[s], v)
                sat[s] = v

    def phase_barrier(self):
        toks = [(s, self.cnt[s]) for s in COMPUTE_SEMS]
        toks += [(s, self.cnt[s]) for s in ALL_DMA_SEMS]
        for e in ("pe", "dve", "act", "pool", "sp"):
            self.raw_wait(e, toks)


def build_full(debug=False):
    nc = bass.Bass("TRN2", target_bir_lowering=False)

    def din(name, shape):
        return nc.dram_tensor(name, shape, F32, kind="ExternalInput").ap()

    x_full = din("x_full", [SEQ, D])
    wqkva_d = din("wqkva", [D, 192])
    wqbaug_d = din("wqbaug", [64, NH * 192])
    qnw_d = din("qnw", [64, 1])
    kvnw_d = din("kvnw", [64, 1])
    ktwt_d = din("ktwt", [128, NH * 128])
    wkvbv_d = din("wkvbv", [64, NH * 64])
    wo_d = din("wo", [NH * 64, D])
    wt_d = din("wt", [D, D])
    agw_d = din("agw", [D, 136])
    bcat_d = din("bcat", [128, D])
    brow_d = din("brow", [1, D])
    gateb_d = din("gateb", [128, 8])
    cosk_d = din("cosk", [64, SEQ])
    sink_d = din("sink", [64, SEQ])
    ident_d = din("ident", [128, 128])
    ones_d = din("ones1", [1, 128])
    sel_d = din("sel8", [8, 512])

    out_d = nc.dram_tensor("out", [NQ, D], F32, kind="ExternalOutput").ap()
    xmid_d = nc.dram_tensor("xmid", [NQ, D], F32,
                            kind="ExternalOutput" if debug else "Internal").ap()
    dbg = {}
    if debug:
        for nm, shp in [("d_cT", [128, SEQ]), ("d_rqT", [64, NQ]),
                        ("d_V0", [128, 1040]), ("d_KT15", [128, SEQ]),
                        ("d_QT15", [128, NQ]), ("d_woL0", [128, NQ]),
                        ("d_woLall", [128, 8 * NQ]), ("d_recB", [64, NQ]),
                        ("d_xaw", [128, NQ])]:
            dbg[nm] = nc.dram_tensor(nm, shp, F32, kind="ExternalOutput").ap()

    with contextlib.ExitStack() as ST:
        E = ST.enter_context
        sems = {}
        for s in ("spe", "sdv", "sac", "spl", "ssp", *ALL_DMA_SEMS):
            sems[s] = E(nc.semaphore(s))
        P = Prog(nc, sems)
        B = P.B

        ident_b = E(nc.sbuf_tensor("ident_b", [128, 128], BF16))
        ones_b = E(nc.sbuf_tensor("ones_b", [1, 128], BF16))
        sel_b = E(nc.sbuf_tensor("sel_b", [8, 512], BF16))
        brow_b = E(nc.sbuf_tensor("brow_b", [1, D], BF16))
        gateb_b = E(nc.sbuf_tensor("gateb_b", [128, 8], F32))
        qnw_b = E(nc.sbuf_tensor("qnw_b", [64, 1], F32))
        kvnw_b = E(nc.sbuf_tensor("kvnw_b", [64, 1], F32))
        eps_b = E(nc.sbuf_tensor("eps_b", [128, 1], F32))
        eps_r = E(nc.sbuf_tensor("eps_r", [128, 1], F32))
        st6b = [E(nc.sbuf_tensor(f"st6b{i}", [128, 24], F32)) for i in range(2)]
        mvb = [E(nc.sbuf_tensor(f"mvb{i}", [128, 2], F32)) for i in range(2)]
        rstdb = [E(nc.sbuf_tensor(f"rstdb{i}", [128, 1], F32)) for i in range(2)]
        nbb = [E(nc.sbuf_tensor(f"nbb{i}", [128, 1], F32)) for i in range(2)]

        blk = E(nc.Block())

        @blk.gpsimd
        def _section(_g):
            ST_WA = contextlib.ExitStack()
            Wt_sb = [ST_WA.enter_context(nc.sbuf_tensor(f"Wt{t}", [128, D], BF16))
                     for t in range(8)]
            ST_wo = contextlib.ExitStack()
            wo_b = ST_wo.enter_context(nc.sbuf_tensor("wo_b", [128, 8 * D], BF16))
            woL = [ST_wo.enter_context(nc.sbuf_tensor(f"woL{p}", [128, NQ], BF16))
                   for p in range(8)]
            ST12 = contextlib.ExitStack()
            E12 = ST12.enter_context
            cT = E12(nc.sbuf_tensor("cT", [128, SEQ], BF16))
            rqT = E12(nc.sbuf_tensor("rqT", [64, NQ], BF16))
            Vsb = [E12(nc.sbuf_tensor(f"V{rt}", [128, 1040], BF16))
                   for rt in range(RT_F)]
            cosk_b = E12(nc.sbuf_tensor("cosk_b", [64, SEQ], BF16))
            sink_b = E12(nc.sbuf_tensor("sink_b", [64, SEQ], BF16))
            ktwt_b = E12(nc.sbuf_tensor("ktwt_b", [128, NH * 128], BF16))
            wqbaug_b = E12(nc.sbuf_tensor("wqbaug_b", [64, NH * 192], BF16))

            ST1 = contextlib.ExitStack()
            E1 = ST1.enter_context
            wqkva_b = E1(nc.sbuf_tensor("wqkva_b", [128, RT_F * 192], BF16))
            wkvbv_b = E1(nc.sbuf_tensor("wkvbv_b", [64, NH * 64], BF16))
            xtmp = [E1(nc.sbuf_tensor(f"xtmp{i}", [128, D], F32)) for i in range(3)]
            lnb = [E1(nc.sbuf_tensor(f"lnb{i}", [128, D], BF16)) for i in range(2)]
            slab = [E1(nc.sbuf_tensor(f"slab{i}", [128, D], BF16)) for i in range(2)]
            prj = [E1(nc.sbuf_tensor(f"prj{i}", [128, 192], BF16)) for i in range(2)]
            sqs = [E1(nc.sbuf_tensor(f"sqs{i}", [128, 64], F32)) for i in range(2)]
            st6 = [E1(nc.sbuf_tensor(f"st6{i}", [128, 24], F32)) for i in range(2)]
            mv = [E1(nc.sbuf_tensor(f"mv{i}", [128, 2], F32)) for i in range(2)]
            rstd = [E1(nc.sbuf_tensor(f"rstd{i}", [128, 1], F32)) for i in range(6)]
            nbias = [E1(nc.sbuf_tensor(f"nbias{i}", [128, 1], F32)) for i in range(2)]
            kpF = E1(nc.sbuf_tensor("kpF", [64, SEQ], BF16))
            krx = E1(nc.sbuf_tensor("krx", [64, SEQ], BF16))

            psP = E1(nc.psum_tensor("psP", [128, 1024], F32))
            psC = E1(nc.psum_tensor("psC", [128, 512], F32))
            psQ = E1(nc.psum_tensor("psQ", [128, 512], F32))
            psV = [E1(nc.psum_tensor(f"psV{i}", [128, 1024], F32))
                   for i in range(2)]

            for name, dst, srcap in [
                ("qnw", qnw_b, qnw_d), ("kvnw", kvnw_b, kvnw_d),
                ("gateb", gateb_b, gateb_d),
            ]:
                P.run("sp", [], [B(name)],
                      lambda e, dst=dst, srcap=srcap: e.dma_start(out=dst[:, :],
                                                                  in_=srcap[:, :]),
                      dma="dx")
            for name, dst, srcap in [
                ("cosk", cosk_b, cosk_d), ("sink", sink_b, sink_d),
                ("wqbaug", wqbaug_b, wqbaug_d), ("ktwt", ktwt_b, ktwt_d),
                ("wkvbv", wkvbv_b, wkvbv_d),
            ]:
                P.run("pool", [], [B(name)],
                      lambda e, dst=dst, srcap=srcap: e.dma_start(out=dst[:, :],
                                                                  in_=srcap[:, :]),
                      dma="dwp")
            P.run("pool", [], [B("eps")],
                  lambda e: e.memset(eps_b[:, :], EPS_LN))
            P.run("pool", [], [B("epsr")],
                  lambda e: e.memset(eps_r[:, :], EPS_RMS))
            P.run("pool", [], [B("ident")],
                  lambda e: e.dma_start(out=ident_b[:, :], in_=ident_d[:, :]),
                  dma="dwp")
            P.run("pool", [], [B("ones")],
                  lambda e: e.dma_start(out=ones_b[:, :], in_=ones_d[:, :]),
                  dma="dwp")
            P.run("pool", [], [B("sel")],
                  lambda e: e.dma_start(out=sel_b[:, :], in_=sel_d[:, :]),
                  dma="dwp")
            P.run("pool", [], [B("brow")],
                  lambda e: e.dma_start(out=brow_b[:, :], in_=brow_d[:, :]),
                  dma="dwp")
            P.run("pool", [], [B("wqkva")],
                  lambda e: e.dma_start(
                      out=wqkva_b[:, :].rearrange("p (t c) -> p t c", c=192),
                      in_=wqkva_d.rearrange("(t p) c -> p t c", p=128)),
                  dma="dwp")

            P.run("dve", [B("wqbaug"), B("qnw")], [B("wqbaug")],
                  lambda e: e.tensor_scalar(wqbaug_b[:, :], wqbaug_b[:, :],
                                            qnw_b[:, 0:1], None, ALU.mult))
            for rt in range(RT_F):
                P.run("pool", [], [B(f"V{rt}")],
                      lambda e, rt=rt: e.memset(Vsb[rt][:, :], 1.0))

            def emit_x_dma(rt):
                P.run("sp", [], [B(f"xtmp{rt % 3}")],
                      lambda e, rt=rt: e.dma_start(
                          out=xtmp[rt % 3][:, :],
                          in_=x_full[rt * 128:(rt + 1) * 128, :]),
                      dma="dx")

            emit_x_dma(0)
            emit_x_dma(1)
            emit_x_dma(2)

            def ln1_stats(rt):
                xb = xtmp[rt % 3]
                i2 = rt % 2
                xB = B(f"xtmp{rt % 3}")
                mvB, r0B = B(f"mv{i2}"), B(f"rstd{i2}")
                for i in range(4):
                    P.run("dve", [xB], [B(f"st6_{i2}_{i}")],
                          lambda e, i=i, i2=i2, xb=xb: e.bn_stats(
                              st6[i2][:, i * 6:(i + 1) * 6],
                              xb[:, i * 512:(i + 1) * 512]))
                P.run("dve", [B(f"st6_{i2}_{i}") for i in range(4)], [mvB],
                      lambda e, i2=i2: e.bn_aggr(mv[i2][:, :], st6[i2][:, :]))

            def ln1_apply(rt):
                xb = xtmp[rt % 3]
                i2 = rt % 2
                xB = B(f"xtmp{rt % 3}")
                lnB = B(f"lnb{i2}")
                mvB, r0B, nbB = B(f"mv{i2}"), B(f"rstd{i2}"), B(f"nbias{i2}")
                P.run("act", [mvB, B("eps"), r0B], [r0B],
                      lambda e, i2=i2: e.activation(rstd[i2][:, :], mv[i2][:, 1:2],
                                                    AF.Ln, bias=eps_b[:, 0:1]))
                P.run("act", [r0B], [r0B],
                      lambda e, i2=i2: e.activation(rstd[i2][:, :], rstd[i2][:, :],
                                                    AF.Exp, scale=-0.5))
                P.run("dve", [mvB, r0B], [nbB],
                      lambda e, i2=i2: e.scalar_tensor_tensor(
                          nbias[i2][:, :], mv[i2][:, 0:1], -1.0, rstd[i2][:, :],
                          ALU.mult, ALU.mult))
                P.run("act", [xB, r0B, nbB], [lnB],
                      lambda e, i2=i2, xb=xb: e.activation(
                          lnb[i2][:, :], xb[:, :], AF.Identity,
                          bias=nbias[i2][:, 0:1], scale=rstd[i2][:, 0:1]))

            ln1_stats(0)
            ln1_apply(0)

            def p1_front(rt):
                i2 = rt % 2
                lnB = B(f"lnb{i2}")
                if rt + 1 < RT_F:
                    ln1_stats(rt + 1)
                if rt + 3 < RT_F:
                    emit_x_dma(rt + 3)
                for bi in range(4):
                    half = bi % 2

                    def mmT(e, bi=bi, half=half):
                        for t in range(4):
                            tt = bi * 4 + t
                            inst = e.matmul(
                                psP[:, half * 512 + t * 128:half * 512 + (t + 1) * 128],
                                lnb[i2][:, tt * 128:(tt + 1) * 128],
                                ident_b[:, :], start=True, stop=True)
                        return inst
                    P.run("pe", [lnB, B("ident")], [B(f"psP{half}")], mmT)
                    if bi % 2 == 0:
                        P.run("act", [B(f"psP{half}")], [B(f"slab{i2}_{bi}")],
                              lambda e, bi=bi, half=half: e.activation(
                                  slab[i2][:, bi * 512:(bi + 1) * 512],
                                  psP[:, half * 512:(half + 1) * 512], AF.Identity))
                    else:
                        P.run("dve", [B(f"psP{half}")], [B(f"slab{i2}_{bi}")],
                              lambda e, bi=bi, half=half: e.tensor_copy(
                                  slab[i2][:, bi * 512:(bi + 1) * 512],
                                  psP[:, half * 512:(half + 1) * 512]))
                if rt + 1 < RT_F:
                    ln1_apply(rt + 1)

            def p1_back1(rt):
                i2 = rt % 2
                prB = B(f"prj{i2}")
                sqB, r1B, r2B = B(f"sqs{i2}"), B(f"rstd{2 + i2}"), B(f"rstd{4 + i2}")
                for g in range(4):
                    def mmC(e, g=g):
                        for t in range(g * 4, (g + 1) * 4):
                            inst = e.matmul(psC[:, 0:192],
                                            slab[i2][:, t * 128:(t + 1) * 128],
                                            wqkva_b[:, t * 192:(t + 1) * 192],
                                            start=(t == 0), stop=(t == RT_F - 1),
                                            skip_group_check=True)
                        return inst
                    P.run("pe", [B(f"slab{i2}_{g}"), B("wqkva")], [B("psC")], mmC)
                P.run("act", [B("psC")], [sqB, r1B],
                      lambda e: e.activation(sqs[i2][:, :], psC[:, 0:64], AF.Square,
                                             accum_out=rstd[2 + i2][:, 0:1]))
                P.run("act", [r1B, B("epsr")], [r1B],
                      lambda e: e.activation(rstd[2 + i2][:, :], rstd[2 + i2][:, :],
                                             AF.Ln, scale=1.0 / 64.0,
                                             bias=eps_r[:, 0:1]))
                P.run("act", [r1B], [r1B],
                      lambda e: e.activation(rstd[2 + i2][:, :], rstd[2 + i2][:, :],
                                             AF.Exp, scale=-0.5))
                P.run("act", [B("psC"), r1B], [prB],
                      lambda e: e.activation(prj[i2][:, 0:64], psC[:, 0:64],
                                             AF.Identity, scale=rstd[2 + i2][:, 0:1]))
                P.run("act", [B("psC")], [prB],
                      lambda e: e.activation(prj[i2][:, 64:128], psC[:, 64:128],
                                             AF.Identity))
                if rt < RT_O:
                    P.run("act", [B("psC")], [sqB, r2B],
                          lambda e: e.activation(sqs[i2][:, :], psC[:, 128:192],
                                                 AF.Square,
                                                 accum_out=rstd[4 + i2][:, 0:1]))
                    P.run("act", [r2B, B("epsr")], [r2B],
                          lambda e: e.activation(rstd[4 + i2][:, :],
                                                 rstd[4 + i2][:, :], AF.Ln,
                                                 scale=1.0 / 64.0,
                                                 bias=eps_r[:, 0:1]))
                    P.run("act", [r2B], [r2B],
                          lambda e: e.activation(rstd[4 + i2][:, :], rstd[4 + i2][:, :],
                                                 AF.Exp, scale=-0.5))
                    P.run("act", [B("psC"), r2B], [prB],
                          lambda e: e.activation(prj[i2][:, 128:192], psC[:, 128:192],
                                                 AF.Identity,
                                                 scale=rstd[4 + i2][:, 0:1]))

            def p1_back2(rt):
                i2 = rt % 2
                prB = B(f"prj{i2}")

                def mmP(e):
                    inst = e.matmul(psQ[:, 0:128], prj[i2][:, 0:128], ident_b[:, :],
                                    start=True, stop=True)
                    if rt < RT_O:
                        inst = e.matmul(psQ[0:64, 128:256], prj[i2][:, 128:192],
                                        ident_b[:, :], start=True, stop=True)
                    return inst
                P.run("pe", [prB, B("ident")], [B("psQ")], mmP)
                P.run("dve", [B("psQ"), B("kvnw")], [B("cT")],
                      lambda e: e.tensor_scalar(cT[0:64, rt * 128:(rt + 1) * 128],
                                                psQ[0:64, 0:128],
                                                kvnw_b[:, 0:1], None, ALU.mult))
                P.run("dve", [B("psQ")], [B("kpF")],
                      lambda e: e.tensor_copy(kpF[:, rt * 128:(rt + 1) * 128],
                                              psQ[64:128, 0:128]))
                if rt < RT_O:
                    P.run("dve", [B("psQ")], [B("rqT")],
                          lambda e: e.tensor_copy(rqT[:, rt * 128:(rt + 1) * 128],
                                                  psQ[0:64, 128:256]))

            def p1_back3(rt):
                pv = psV[rt % 2]

                def mmV(e):
                    e.matmul(pv[:, 0:512], cT[0:64, rt * 128:(rt + 1) * 128],
                             wkvbv_b[:, 0:512], start=True, stop=True)
                    return e.matmul(pv[:, 512:1024],
                                    cT[0:64, rt * 128:(rt + 1) * 128],
                                    wkvbv_b[:, 512:1024], start=True, stop=True)
                P.run("pe", [B("cT"), B("wkvbv")], [B(f"psV{rt % 2}")], mmV)
                P.run("act", [B(f"psV{rt % 2}")], [B(f"V{rt}")],
                      lambda e: e.activation(
                          Vsb[rt][:, :].rearrange("p (h c) -> p h c", c=65)[:, :, 0:64],
                          pv[:, 0:1024].rearrange("p (h c) -> p h c", c=64),
                          AF.Identity))

            for rt in range(RT_F):
                p1_front(rt)
                if rt >= 1:
                    p1_back1(rt - 1)
                if rt >= 2:
                    p1_back2(rt - 2)
                if rt >= 3:
                    p1_back3(rt - 3)
            p1_back1(RT_F - 1)
            p1_back2(RT_F - 2)
            p1_back3(RT_F - 3)
            p1_back2(RT_F - 1)
            p1_back3(RT_F - 2)
            p1_back3(RT_F - 1)

            P.run("dve", [B("kpF")], [B("krx")],
                  lambda e: e.tensor_copy(krx[0:32, :], kpF[32:64, :]))
            P.run("dve", [B("kpF")], [B("krx")],
                  lambda e: e.tensor_copy(krx[32:64, :], kpF[0:32, :]))
            P.run("dve", [B("krx"), B("sink")], [B("krx")],
                  lambda e: e.tensor_tensor(krx[:, :], krx[:, :], sink_b[:, :],
                                            ALU.mult))
            P.run("dve", [B("kpF"), B("cosk")], [B("kpF")],
                  lambda e: e.tensor_tensor(kpF[:, :], kpF[:, :], cosk_b[:, :],
                                            ALU.mult))
            P.run("dve", [B("kpF"), B("krx")], [B("cT")],
                  lambda e: e.tensor_tensor(cT[64:128, :], kpF[:, :], krx[:, :],
                                            ALU.add))

            P.phase_barrier()
            ST1.close()

            P.run("pool", [], [B("wo_b")],
                  lambda e: e.dma_start(
                      out=wo_b[:, :].rearrange("p (t c) -> p t c", c=D),
                      in_=wo_d.rearrange("(t p) c -> p t c", p=128)), dma="dwp")
            for t in range(8):
                P.run("pool", [], [B(f"Wt{t}")],
                      lambda e, t=t: e.dma_start(out=Wt_sb[t][:, :],
                                                 in_=wt_d[t * 128:(t + 1) * 128, :]),
                      dma="dwp")

            ST2 = contextlib.ExitStack()
            E2 = ST2.enter_context
            QX = [E2(nc.sbuf_tensor(f"QX{i}", [128, NQ], BF16)) for i in range(2)]
            QT = [E2(nc.sbuf_tensor(f"QT{i}", [128, NQ], BF16)) for i in range(2)]
            eS = [E2(nc.sbuf_tensor(f"eS{i}", [128, NQ], BF16)) for i in range(6)]
            den_r = E2(nc.sbuf_tensor("den_r", [1, NQ], BF16))
            recT_b = E2(nc.sbuf_tensor("recT_b", [128, 8], BF16))
            rec8_b = E2(nc.sbuf_tensor("rec8_b", [8, 128], BF16))
            recB = E2(nc.sbuf_tensor("recB", [64, NQ], BF16))
            u_s = E2(nc.sbuf_tensor("u_s", [64, 512], F32))
            t_s = E2(nc.sbuf_tensor("t_s", [64, 512], F32))

            psS = [E2(nc.psum_tensor(f"psS{i}", [128, 1024], F32)) for i in range(2)]
            psO = E2(nc.psum_tensor("psO", [128, 1024], F32))
            psPr = E2(nc.psum_tensor("psPr", [128, 1024], F32))

            def emit_head_prod_qx(h):
                hb = h % 2
                qxB = B(f"QX{hb}")
                for c in range(2):
                    half = c % 2
                    P.run("pe", [B("ktwt"), B(f"QT{hb}")], [B(f"psPr{half}")],
                          lambda e, c=c, half=half, h=h, hb=hb: e.matmul(
                              psPr[:, half * 512:(half + 1) * 512],
                              ktwt_b[:, h * 128:(h + 1) * 128],
                              QT[hb][:, c * 512:(c + 1) * 512],
                              start=True, stop=True))
                    P.run("dve", [B(f"psPr{half}")], [qxB],
                          lambda e, c=c, half=half, hb=hb: e.tensor_copy(
                              QX[hb][:, c * 512:(c + 1) * 512],
                              psPr[:, half * 512:(half + 1) * 512]))

            def emit_head_prod_q(h, c):
                hb = h % 2
                qtB = B(f"QT{hb}")

                def mmQ(e, c=c, h=h):
                    e.matmul(psPr[:, 0:512],
                             wqbaug_b[:, h * 192:h * 192 + 128],
                             rqT[:, c * 512:(c + 1) * 512], start=True, stop=True)
                    return e.matmul(psPr[0:64, 512:1024],
                                    wqbaug_b[:, h * 192 + 128:h * 192 + 192],
                                    rqT[:, c * 512:(c + 1) * 512],
                                    start=True, stop=True)
                P.run("pe", [B("wqbaug"), B("rqT")],
                      [B("psPr0"), B("psPr1")], mmQ)
                P.run("dve", [B("psPr0")], [qtB],
                      lambda e, c=c, hb=hb: e.tensor_copy(
                          QT[hb][0:64, c * 512:(c + 1) * 512],
                          psPr[0:64, 0:512]))
                P.run("dve", [B("psPr0"), B("cosk")], [B("u_s")],
                      lambda e, c=c: e.tensor_tensor(
                          u_s[:, :], psPr[64:128, 0:512],
                          cosk_b[:, c * 512:(c + 1) * 512], ALU.mult))
                P.run("dve", [B("psPr1"), B("sink")], [B("t_s")],
                      lambda e, c=c: e.tensor_tensor(
                          t_s[:, :], psPr[0:64, 512:1024],
                          sink_b[:, c * 512:(c + 1) * 512], ALU.mult))
                P.run("dve", [B("u_s"), B("t_s")], [qtB],
                      lambda e, c=c, hb=hb: e.tensor_tensor(
                          QT[hb][64:128, c * 512:(c + 1) * 512],
                          u_s[:, :], t_s[:, :], ALU.add))

            def emit_head_prod(h):
                emit_head_prod_q(h, 0)
                emit_head_prod_q(h, 1)
                emit_head_prod_qx(h)

            def head_den(h):
                def mmDT(e):
                    for c in range(8):
                        inst = e.matmul(psPr[:, c:c + 1],
                                        den_r[0:1, c * 128:(c + 1) * 128],
                                        ones_b[0:1, 0:1], start=True, stop=True)
                    return inst
                P.run("pe", [B("denr"), B("ones")], [B("psPr0")], mmDT)

                def _recip(e):
                    with nc.allow_low_precision(reason="1/denom stored bf16 for the broadcast matmul"):
                        return e.reciprocal(recT_b[:, :], psPr[:, 0:8])
                P.run("dve", [B("psPr0")], [B("recT")], _recip)

            def head_tail_a(h):
                def mm8(e):
                    return e.matmul(psPr[0:8, 128:256], recT_b[:, :],
                                    ident_b[:, :], start=True, stop=True)
                P.run("pe", [B("recT"), B("ident")], [B("psPr0")], mm8)
                P.run("dve", [B("psPr0")], [B("rec8")],
                      lambda e: e.tensor_copy(rec8_b[:, :], psPr[0:8, 128:256]))

            def head_tail_b(h):
                def mmBC(e):
                    for c in range(8):
                        inst = e.matmul(psPr[0:64, c * 128:(c + 1) * 128],
                                        sel_b[0:8, c * 64:(c + 1) * 64],
                                        rec8_b[0:8, 0:128], start=True, stop=True)
                    return inst
                P.run("pe", [B("sel"), B("rec8")], [B("psPr0"), B("psPr1")], mmBC)
                P.run("dve", [B("psPr0"), B("psPr1")], [B("recB")],
                      lambda e: e.tensor_copy(recB[:, :], psPr[0:64, 0:1024]))
                p, phalf = h // 2, h % 2
                P.run("dve", [B("psO"), B("recB")], [B(f"woL{p}")],
                      lambda e, p=p, phalf=phalf: e.tensor_tensor(
                          woL[p][phalf * 64:(phalf + 1) * 64, :],
                          psO[0:64, 0:1024], recB[:, :], ALU.mult))

            emit_head_prod(0)
            for h in range(NH):
                hb = h % 2
                qxB = B(f"QX{hb}")

                def emit_av(kt):
                    kb = kt % 6

                    def mmAV(e, kt=kt, kb=kb):
                        e.matmul(psO[0:65, 0:512], Vsb[kt][:, h * 65:(h + 1) * 65],
                                 eS[kb][:, 0:512], start=(kt == 0),
                                 stop=(kt == RT_F - 1), skip_group_check=True)
                        return e.matmul(psO[0:65, 512:1024],
                                        Vsb[kt][:, h * 65:(h + 1) * 65],
                                        eS[kb][:, 512:1024], start=(kt == 0),
                                        stop=(kt == RT_F - 1), skip_group_check=True)
                    P.run("pe", [B(f"V{kt}"), B(f"eS{kb}")], [B("psO")], mmAV)

                for kt in range(RT_F):
                    kb = kt % 6
                    ks = kt % 2

                    def mmS(e, kt=kt, ks=ks):
                        e.matmul(psS[ks][:, 0:512],
                                 cT[:, kt * 128:(kt + 1) * 128],
                                 QX[hb][:, 0:512], start=True, stop=True)
                        return e.matmul(psS[ks][:, 512:1024],
                                        cT[:, kt * 128:(kt + 1) * 128],
                                        QX[hb][:, 512:1024], start=True, stop=True)
                    P.run("pe", [B("cT"), qxB], [B(f"psS{ks}")], mmS)
                    P.run("act", [B(f"psS{ks}")], [B(f"eS{kb}")],
                          lambda e, kb=kb, ks=ks: e.activation(eS[kb][:, :],
                                                               psS[ks][:, 0:1024],
                                                               AF.Exp, scale=SCALE))
                    if h > 0:
                        if kt == 0:
                            head_den(h - 1)
                        elif kt == 1:
                            head_tail_a(h - 1)
                        elif kt == 2:
                            head_tail_b(h - 1)
                    if kt >= 5:
                        emit_av(kt - 5)
                    if h + 1 < NH:
                        if kt == 7:
                            emit_head_prod_q(h + 1, 0)
                        elif kt == 10:
                            emit_head_prod_q(h + 1, 1)
                        elif kt == 13:
                            emit_head_prod_qx(h + 1)
                emit_av(RT_F - 5)
                emit_av(RT_F - 4)
                emit_av(RT_F - 3)
                emit_av(RT_F - 2)
                emit_av(RT_F - 1)
                P.run("act", [B("psO")], [B("denr")],
                      lambda e: e.activation(den_r[0:1, :], psO[64:65, 0:1024],
                                             AF.Identity))
            head_den(NH - 1)
            head_tail_a(NH - 1)
            head_tail_b(NH - 1)

            if debug:
                for nm, tile, bb in [("d_cT", cT, B("cT")), ("d_KT15", QX[1], B("QX1")),
                                     ("d_QT15", QT[1], B("QT1")),
                                     ("d_woL0", woL[0], B("woL0"))]:
                    P.run("pool", [bb], [],
                          lambda e, nm=nm, tile=tile: e.dma_start(
                              out=dbg[nm][:, 0:tile.shape[1]], in_=tile[:, :]),
                          dma="dso")
                P.run("pool", [B("rqT")], [],
                      lambda e: e.dma_start(out=dbg["d_rqT"][:, :], in_=rqT[:, :]),
                      dma="dso")
                P.run("pool", [B("V0")], [],
                      lambda e: e.dma_start(out=dbg["d_V0"][:, :], in_=Vsb[0][:, :]),
                      dma="dso")
                for p in range(8):
                    P.run("pool", [B(f"woL{p}")], [],
                          lambda e, p=p: e.dma_start(
                              out=dbg["d_woLall"][:, p * NQ:(p + 1) * NQ],
                              in_=woL[p][:, :]), dma="dso")
                P.run("pool", [B("recB")], [],
                      lambda e: e.dma_start(out=dbg["d_recB"][:, :], in_=recB[:, :]),
                      dma="dso")

            P.phase_barrier()
            ST2.close()
            ST12.close()

            ST34 = contextlib.ExitStack()
            E4 = ST34.enter_context
            xre = [E4(nc.sbuf_tensor(f"xre{i}", [128, D], F32)) for i in range(2)]
            lnT2 = [E4(nc.sbuf_tensor(f"lnT2_{q}", [128, D], BF16))
                    for q in range(RT_O)]
            xaT = E4(nc.sbuf_tensor("xaT", [128, RT_O * 128], BF16))
            agw_b = E4(nc.sbuf_tensor("agw_b", [128, RT_F * 136], BF16))
            st6c = [E4(nc.sbuf_tensor(f"st6c{i}", [128, 24], F32)) for i in range(2)]
            mvc = [E4(nc.sbuf_tensor(f"mvc{i}", [128, 2], F32)) for i in range(2)]
            rstdc = [E4(nc.sbuf_tensor(f"rstdc{i}", [128, 1], F32)) for i in range(2)]
            nbc = [E4(nc.sbuf_tensor(f"nbc{i}", [128, 1], F32)) for i in range(2)]
            lg = E4(nc.sbuf_tensor("lg", [128, 8], F32))
            m1 = E4(nc.sbuf_tensor("m1", [128, 1], F32))
            m2 = E4(nc.sbuf_tensor("m2", [128, 1], F32))
            msk1 = E4(nc.sbuf_tensor("msk1", [128, 8], F32))
            msk2 = E4(nc.sbuf_tensor("msk2", [128, 8], F32))
            l2t = E4(nc.sbuf_tensor("l2t", [128, 8], F32))
            g1 = E4(nc.sbuf_tensor("g1", [128, 1], F32))
            g2 = E4(nc.sbuf_tensor("g2", [128, 1], F32))
            w8 = E4(nc.sbuf_tensor("w8", [128, 8], F32))
            e2s = E4(nc.sbuf_tensor("e2s", [128, 1], F32))
            xaw = [E4(nc.sbuf_tensor(f"xaw{i}", [128, 128], BF16))
                   for i in range(2)]
            w128 = E4(nc.sbuf_tensor("w128", [128, 128], F32))
            for t in range(8, RT_F):
                Wt_sb.append(E4(nc.sbuf_tensor(f"Wt{t}", [128, D], BF16)))
            ST34b = contextlib.ExitStack()
            ln2b = [ST34b.enter_context(nc.sbuf_tensor(f"ln2b{q}", [128, D], BF16))
                    for q in range(RT_O)]

            for t in range(8, RT_F):
                P.run("pool", [], [B(f"Wt{t}")],
                      lambda e, t=t: e.dma_start(out=Wt_sb[t][:, :],
                                                 in_=wt_d[t * 128:(t + 1) * 128, :]),
                      dma="dwp")
            P.run("pool", [], [B("agw")],
                  lambda e: e.dma_start(
                      out=agw_b[:, :].rearrange("p (t c) -> p t c", c=136),
                      in_=agw_d.rearrange("(t p) c -> p t c", p=128)), dma="dwp")

            ST3p = contextlib.ExitStack()
            psW = [ST3p.enter_context(nc.psum_tensor(f"psW{i}", [128, 2048], F32))
                   for i in range(2)]

            def emit_ln(qb, srcbuf, src_ap, dve_src_ap, outbuf, out_ap):
                for i in range(4):
                    P.run("dve", [srcbuf], [B(f"st6b{qb}_{i}")],
                          lambda e, i=i: e.bn_stats(st6b[qb][:, i * 6:(i + 1) * 6],
                                                    dve_src_ap(i)))
                P.run("dve", [B(f"st6b{qb}_{i}") for i in range(4)], [B(f"mvb{qb}")],
                      lambda e: e.bn_aggr(mvb[qb][:, :], st6b[qb][:, :]))
                P.run("act", [B(f"mvb{qb}"), B("eps")], [B(f"rstdb{qb}")],
                      lambda e: e.activation(rstdb[qb][:, :], mvb[qb][:, 1:2],
                                             AF.Ln, bias=eps_b[:, 0:1]))
                P.run("act", [B(f"rstdb{qb}")], [B(f"rstdb{qb}")],
                      lambda e: e.activation(rstdb[qb][:, :], rstdb[qb][:, :],
                                             AF.Exp, scale=-0.5))
                P.run("dve", [srcbuf, B(f"mvb{qb}"), B(f"rstdb{qb}")], [outbuf],
                      lambda e: e.tensor_scalar(out_ap, src_ap,
                                                mvb[qb][:, 0:1],
                                                rstdb[qb][:, 0:1],
                                                ALU.subtract, op1=ALU.mult))

            def wo_xre_dma(q):
                P.run("sp", [], [B(f"xre{q % 2}")],
                      lambda e, q=q: e.dma_start(
                          out=xre[q % 2][:, :], in_=x_full[q * 128:(q + 1) * 128, :]),
                      dma="dx")

            wo_xre_dma(0)
            for q in range(RT_O):
                qb = q % 2
                if q + 1 < RT_O:
                    wo_xre_dma(q + 1)

                def mmWO(e, q=q, qb=qb):
                    for p in range(8):
                        for c in range(4):
                            inst = e.matmul(
                                psW[qb][:, c * 512:(c + 1) * 512],
                                woL[p][:, q * 128:(q + 1) * 128],
                                wo_b[:, p * D + c * 512:p * D + (c + 1) * 512],
                                start=(p == 0), stop=(p == 7),
                                skip_group_check=True)
                    return inst
                P.run("pe", [B("wo_b")] + [B(f"woL{p}") for p in range(8)],
                      [B(f"psW{qb}")], mmWO)
                emit_ln(qb, B(f"psW{qb}"), psW[qb][:, :],
                        lambda i, qb=qb: psW[qb][:, i * 512:(i + 1) * 512],
                        B(f"psW{qb}"), psW[qb][:, :])
                P.run("dve", [B(f"xre{qb}"), B(f"psW{qb}")], [B(f"xre{qb}")],
                      lambda e, qb=qb: e.tensor_tensor(xre[qb][:, :], xre[qb][:, :],
                                                       psW[qb][:, :], ALU.add))
                if q >= RT_O - 2:
                    cadd = P.cnt["sdv"]
                P.run("pool", [B(f"xre{qb}")], [B(f"xmid{q}")],
                      lambda e, q=q, qb=qb: e.dma_start(
                          out=xmid_d[q * 128:(q + 1) * 128, :], in_=xre[qb][:, :]),
                      dma="dso")
                for i in range(4):
                    P.run("dve", [B(f"xre{qb}")], [B(f"st6c{qb}_{i}")],
                          lambda e, i=i, qb=qb: e.bn_stats(
                              st6c[qb][:, i * 6:(i + 1) * 6],
                              xre[qb][:, i * 512:(i + 1) * 512]))
                P.run("dve", [B(f"st6c{qb}_{i}") for i in range(4)], [B(f"mvc{qb}")],
                      lambda e, qb=qb: e.bn_aggr(mvc[qb][:, :], st6c[qb][:, :]))
                P.run("act", [B(f"mvc{qb}"), B("eps")], [B(f"rstdc{qb}")],
                      lambda e, qb=qb: e.activation(rstdc[qb][:, :], mvc[qb][:, 1:2],
                                                    AF.Ln, bias=eps_b[:, 0:1]))
                P.run("act", [B(f"rstdc{qb}")], [B(f"rstdc{qb}")],
                      lambda e, qb=qb: e.activation(rstdc[qb][:, :], rstdc[qb][:, :],
                                                    AF.Exp, scale=-0.5))
                P.run("dve", [B(f"mvc{qb}"), B(f"rstdc{qb}")], [B(f"nbc{qb}")],
                      lambda e, qb=qb: e.scalar_tensor_tensor(
                          nbc[qb][:, :], mvc[qb][:, 0:1], -1.0, rstdc[qb][:, :],
                          ALU.mult, ALU.mult))
                P.run("act", [B(f"xre{qb}"), B(f"rstdc{qb}"), B(f"nbc{qb}")],
                      [B(f"ln2b{q}")],
                      lambda e, q=q, qb=qb: e.activation(
                          ln2b[q][:, :], xre[qb][:, :], AF.Identity,
                          bias=nbc[qb][:, 0:1], scale=rstdc[qb][:, 0:1]))
                if q == RT_O - 2:
                    c6 = cadd

            c7 = cadd
            ST3p.close()

            ST4p = contextlib.ExitStack()
            psT2 = ST4p.enter_context(nc.psum_tensor("psT2", [128, 1024], F32))
            psT2b = ST4p.enter_context(nc.psum_tensor("psT2b", [128, 1024], F32))
            psA2 = [ST4p.enter_context(nc.psum_tensor(f"psA2_{i}", [128, 512], F32))
                    for i in range(2)]
            psXT = ST4p.enter_context(nc.psum_tensor("psXT", [128, 512], F32))
            P.raw_wait("pe", [("sdv", c6)])

            def mmT2_grp(q, bi):
                pst = psT2 if bi % 2 == 0 else psT2b
                half = (bi // 2) % 2

                def mm(e, q=q, bi=bi, pst=pst, half=half):
                    for t in range(4):
                        tt = bi * 4 + t
                        inst = e.matmul(
                            pst[:, half * 512 + t * 128:half * 512 + (t + 1) * 128],
                            ln2b[q][:, tt * 128:(tt + 1) * 128],
                            ident_b[:, :], start=True, stop=True)
                    return inst
                P.run("pe", [B(f"ln2b{q}"), B("ident")], [B(f"psT2{bi % 4}")], mm)

            def evac(q, bi, eng):
                pst = psT2 if bi % 2 == 0 else psT2b
                half = (bi // 2) % 2
                if eng == "act":
                    P.run("act", [B(f"psT2{bi % 4}")], [B(f"lnT2_{q}_{bi}")],
                          lambda e, q=q, bi=bi, pst=pst, half=half: e.activation(
                              lnT2[q][:, bi * 512:(bi + 1) * 512],
                              pst[:, half * 512:(half + 1) * 512], AF.Identity))
                else:
                    P.run("dve", [B(f"psT2{bi % 4}")], [B(f"lnT2_{q}_{bi}")],
                          lambda e, q=q, bi=bi, pst=pst, half=half: e.tensor_copy(
                              lnT2[q][:, bi * 512:(bi + 1) * 512],
                              pst[:, half * 512:(half + 1) * 512]))

            def mmA_grp(q, g):
                def mm(e, q=q, g=g):
                    for t in range(g * 4, (g + 1) * 4):
                        inst = e.matmul(psA2[q % 2][:, 0:136],
                                        lnT2[q][:, t * 128:(t + 1) * 128],
                                        agw_b[:, t * 136:(t + 1) * 136],
                                        start=(t == 0), stop=(t == RT_F - 1),
                                        skip_group_check=True)
                    return inst
                P.run("pe", [B(f"lnT2_{q}_{g}"), B("agw")],
                      [B(f"psA2_{q % 2}")], mm)

            def g_lg(q):
                P.run("dve", [B(f"psA2_{q % 2}"), B("gateb")], [B("lg")],
                      lambda e, q=q: e.tensor_tensor(lg[:, :],
                                                     psA2[q % 2][:, 128:136],
                                                     gateb_b[:, :], ALU.add))

            def g_m1(q):
                P.run("dve", [B("lg")], [B("m1")],
                      lambda e: e.reduce_max(m1[:, :], lg[:, :], axis=AX.X))

            def g_msk1(q):
                P.run("dve", [B("lg"), B("m1")], [B("msk1")],
                      lambda e: e.tensor_scalar(msk1[:, :], lg[:, :], m1[:, 0:1],
                                                None, ALU.is_ge))

            def g_l2t(q):
                P.run("dve", [B("msk1"), B("lg")], [B("l2t")],
                      lambda e: e.scalar_tensor_tensor(l2t[:, :], msk1[:, :], -1e9,
                                                       lg[:, :], ALU.mult, ALU.add))

            def g_m2(q):
                P.run("dve", [B("l2t")], [B("m2")],
                      lambda e: e.reduce_max(m2[:, :], l2t[:, :], axis=AX.X))

            def g_msk2(q):
                P.run("dve", [B("l2t"), B("m2")], [B("msk2")],
                      lambda e: e.tensor_scalar(msk2[:, :], l2t[:, :], m2[:, 0:1],
                                                None, ALU.is_ge))

            def g_e2s(q):
                P.run("dve", [B("m2"), B("m1")], [B("e2s")],
                      lambda e: e.tensor_tensor(e2s[:, :], m2[:, :], m1[:, :],
                                                ALU.subtract))
                P.run("act", [B("e2s")], [B("e2s")],
                      lambda e: e.activation(e2s[:, :], e2s[:, :], AF.Exp))

            def g_tail(q):
                P.run("dve", [B("e2s")], [B("g1")],
                      lambda e: e.tensor_scalar(g1[:, :], e2s[:, :], 1.0, None,
                                                ALU.add))
                P.run("dve", [B("g1")], [B("g1")],
                      lambda e: e.reciprocal(g1[:, :], g1[:, :]))
                P.run("dve", [B("e2s"), B("g1")], [B("g2")],
                      lambda e: e.tensor_tensor(g2[:, :], e2s[:, :], g1[:, :],
                                                ALU.mult))
                P.run("dve", [B("msk1"), B("g1")], [B("w8")],
                      lambda e: e.tensor_scalar(w8[:, :], msk1[:, :], g1[:, 0:1],
                                                None, ALU.mult))
                P.run("dve", [B("msk2"), B("g2"), B("w8")], [B("w8")],
                      lambda e: e.scalar_tensor_tensor(w8[:, :], msk2[:, :],
                                                       g2[:, 0:1], w8[:, :],
                                                       ALU.mult, ALU.add))
                P.run("dve", [B("w8")], [B("w128")],
                      lambda e: e.tensor_copy(
                          w128[:, :].rearrange("p (a b) -> p a b", a=8),
                          bass.AP(w8, 0, [[8, 128], [1, 8], [0, 16]])))
                P.run("dve", [B(f"psA2_{q % 2}"), B("w128")], [B(f"xaw{q % 2}")],
                      lambda e, q=q: e.tensor_tensor(
                          xaw[q % 2][:, :], psA2[q % 2][:, 0:128], w128[:, :],
                          ALU.mult))

            def backT(q):
                P.run("pe", [B(f"xaw{q % 2}"), B("ident")], [B("psXT")],
                      lambda e, q=q: e.matmul(psXT[:, 0:128], xaw[q % 2][:, :],
                                              ident_b[:, :], start=True, stop=True))

            def backTevac(q):
                P.run("dve", [B("psXT")], [B("xaT")],
                      lambda e, q=q: e.tensor_copy(xaT[:, q * 128:(q + 1) * 128],
                                                   psXT[:, 0:128]))

            for q in range(RT_O):
                if q == 1:
                    P.raw_wait("pe", [("sdv", c7)])
                if q >= 1:
                    for g in range(4):
                        mmA_grp(q - 1, g)
                mmT2_grp(q, 0)
                evac(q, 0, "act")
                if q >= 1:
                    g_lg(q - 1)
                mmT2_grp(q, 1)
                if q >= 2:
                    backT(q - 2)
                evac(q, 1, "dve")
                if q >= 1:
                    g_m1(q - 1)
                mmT2_grp(q, 2)
                evac(q, 2, "act")
                if q >= 2:
                    backTevac(q - 2)
                if q >= 1:
                    g_msk1(q - 1)
                    g_l2t(q - 1)
                mmT2_grp(q, 3)
                evac(q, 3, "dve")
                if q >= 1:
                    g_m2(q - 1)
                    g_msk2(q - 1)
                    g_e2s(q - 1)
                    g_tail(q - 1)
            for g in range(4):
                mmA_grp(RT_O - 1, g)
            g_lg(RT_O - 1)
            g_m1(RT_O - 1)
            backT(RT_O - 2)
            backTevac(RT_O - 2)
            g_msk1(RT_O - 1)
            g_l2t(RT_O - 1)
            g_m2(RT_O - 1)
            g_msk2(RT_O - 1)
            g_e2s(RT_O - 1)
            g_tail(RT_O - 1)
            backT(RT_O - 1)
            backTevac(RT_O - 1)
            if debug:
                P.run("pool", [B("xaT")], [],
                      lambda e: e.dma_start(out=dbg["d_xaw"][:, 0:1024],
                                            in_=xaT[:, 0:1024]), dma="dso")

            cp2 = (P.cnt["spe"], P.cnt["sdv"], P.cnt["sac"])
            ST4p.close()
            ST34b.close()

            ST34c = contextlib.ExitStack()
            bcat_b = ST34c.enter_context(nc.sbuf_tensor("bcat_b", [128, D], BF16))
            P.raw_wait("pool", [("spe", cp2[0])])
            P.run("pool", [], [B("bcat")],
                  lambda e: e.dma_start(out=bcat_b[:, :], in_=bcat_d[:, :]),
                  dma="dwp")
            ST5 = contextlib.ExitStack()
            psF = [ST5.enter_context(nc.psum_tensor(f"psF{i}", [128, 2048], F32))
                   for i in range(2)]
            P.raw_wait("pe", [("sdv", cp2[1]), ("sac", cp2[2])])

            def w_xre_dma(q):
                P.run("sp", [B(f"xmid{q}")], [B(f"xre{q % 2}")],
                      lambda e, q=q: e.dma_start(
                          out=xre[q % 2][:, :], in_=xmid_d[q * 128:(q + 1) * 128, :]),
                      dma="dx")

            w_xre_dma(0)
            for q in range(RT_O):
                qb = q % 2
                if q + 1 < RT_O:
                    w_xre_dma(q + 1)
                for t in range(RT_F):
                    def mmW(e, t=t, q=q, qb=qb):
                        for c in range(4):
                            inst = e.matmul(psF[qb][:, c * 512:(c + 1) * 512],
                                            lnT2[q][:, t * 128:(t + 1) * 128],
                                            Wt_sb[t][:, c * 512:(c + 1) * 512],
                                            start=(t == 0), stop=False,
                                            skip_group_check=True)
                        return inst
                    P.run("pe", [B(f"lnT2_{q}_{t // 4}"), B(f"Wt{t}")],
                          [B(f"psF{qb}")], mmW)

                def mmLB(e, q=q, qb=qb):
                    for c in range(4):
                        e.matmul(psF[qb][:, c * 512:(c + 1) * 512],
                                 xaT[:, q * 128:(q + 1) * 128],
                                 bcat_b[:, c * 512:(c + 1) * 512],
                                 start=False, stop=False, skip_group_check=True)
                    for c in range(4):
                        inst = e.matmul(psF[qb][:, c * 512:(c + 1) * 512],
                                        ones_b[0:1, :],
                                        brow_b[0:1, c * 512:(c + 1) * 512],
                                        start=False, stop=True,
                                        skip_group_check=True)
                    return inst
                P.run("pe", [B("xaT"), B("bcat"), B("ones"), B("brow")],
                      [B(f"psF{qb}")], mmLB)
                emit_ln(qb, B(f"psF{qb}"), psF[qb][:, :],
                        lambda i, qb=qb: psF[qb][:, i * 512:(i + 1) * 512],
                        B(f"psF{qb}"), psF[qb][:, :])
                P.run("dve", [B(f"xre{qb}"), B(f"psF{qb}")], [B(f"xre{qb}")],
                      lambda e, qb=qb: e.tensor_tensor(xre[qb][:, :], xre[qb][:, :],
                                                       psF[qb][:, :], ALU.add))
                P.run("pool", [B(f"xre{qb}")], [],
                      lambda e, q=q, qb=qb: e.dma_start(
                          out=out_d[q * 128:(q + 1) * 128, :], in_=xre[qb][:, :]),
                      dma="dso")

            P.raw_wait("pool", [(s, P.cnt[s]) for s in DMA_POOLS["dso"]])
            ST5.close()
            ST34c.close()
            ST34.close()
            ST_wo.close()
            ST_WA.close()

    return nc



def make_in_maps(ins):
    hs = np.ascontiguousarray(np.asarray(ins["L0_attn_hidden_states"], dtype=np.float32))
    wq_a = np.asarray(ins["L0_attn_wq_a"], dtype=np.float32)
    qnw = np.asarray(ins["L0_attn_q_a_norm_w"], dtype=np.float32)
    wq_b = np.asarray(ins["L0_attn_wq_b"], dtype=np.float32)
    wkv_a = np.asarray(ins["L0_attn_wkv_a"], dtype=np.float32)
    kvnw = np.asarray(ins["L0_attn_kv_a_norm_w"], dtype=np.float32)
    wkv_b = np.asarray(ins["L0_attn_wkv_b"], dtype=np.float32)
    wo = np.asarray(ins["L0_attn_wo"], dtype=np.float32)
    W = np.asarray(ins["L0_ffn_W"], dtype=np.float32)
    bffn = np.asarray(ins["L0_ffn_b"], dtype=np.float32)
    lora_A = np.asarray(ins["L0_ffn_lora_A"], dtype=np.float32)
    lora_B = np.asarray(ins["L0_ffn_lora_B"], dtype=np.float32)
    gate_w = np.asarray(ins["L0_ffn_gate_w"], dtype=np.float32)
    gate_b = np.asarray(ins["L0_ffn_gate_b"], dtype=np.float32)

    cosk, sink = _yarn_cos_sin()

    wqkva = np.concatenate([wkv_a[:, 0:64], wkv_a[:, 64 + _PERM], wq_a], axis=1)

    wqbaug = np.empty((64, NH * 192), np.float32)
    for h in range(NH):
        base = h * 128
        wqbaug[:, h * 192:h * 192 + 64] = wq_b[:, base:base + 64]
        wqbaug[:, h * 192 + 64:h * 192 + 128] = wq_b[:, base + 64 + _PERM]
        wqbaug[:, h * 192 + 128:h * 192 + 192] = wq_b[:, base + 64 + _PERM_ROT]

    ktwt = np.zeros((128, NH * 128), np.float32)
    eye64 = np.eye(64, dtype=np.float32)
    for h in range(NH):
        ktwt[0:64, h * 128:h * 128 + 64] = wkv_b[:, h * 128:h * 128 + 64].T
        ktwt[64:128, h * 128 + 64:h * 128 + 128] = eye64

    wkvbv = np.empty((64, NH * 64), np.float32)
    for h in range(NH):
        wkvbv[:, h * 64:(h + 1) * 64] = wkv_b[:, h * 128 + 64:h * 128 + 128]

    agw = np.concatenate([lora_A.transpose(2, 0, 1).reshape(D, 128), gate_w.T],
                         axis=1)
    bcat = np.ascontiguousarray(lora_B.transpose(0, 2, 1).reshape(128, D))
    wt = np.ascontiguousarray(W.T)
    shared = dict(
        wqkva=wqkva, wqbaug=wqbaug,
        qnw=np.ascontiguousarray(qnw.reshape(64, 1)),
        kvnw=np.ascontiguousarray(kvnw.reshape(64, 1)),
        ktwt=ktwt, wkvbv=wkvbv, wo=wo, wt=wt, agw=agw, bcat=bcat,
        brow=bffn.reshape(1, D),
        gateb=np.broadcast_to(gate_b.reshape(1, 8), (128, 8)),
        ident=np.eye(128, dtype=np.float32),
        ones1=np.ones((1, 128), np.float32),
        sel8=np.repeat(np.eye(8, dtype=np.float32), 64, axis=1))
    shared = {k: np.ascontiguousarray(v, dtype=np.float32) for k, v in shared.items()}

    in_maps = []
    for c in range(8):
        b, half = c // 2, c % 2
        order = np.arange(SEQ) if half == 0 else np.r_[NQ:SEQ, 0:NQ]
        m = dict(shared)
        m["x_full"] = np.ascontiguousarray(hs[b][order])
        m["cosk"] = np.ascontiguousarray(cosk[:, order])
        m["sink"] = np.ascontiguousarray(sink[:, order])
        in_maps.append(m)
    return in_maps


def kernel(**ins):
    in_maps = make_in_maps(ins)
    nc = build_full(debug=False)
    res = run_bass_kernel_spmd(nc, in_maps, core_ids=list(range(8)))
    out = np.empty((NB, SEQ, D), np.float32)
    for c in range(8):
        b, half = c // 2, c % 2
        out[b, half * NQ:(half + 1) * NQ, :] = np.asarray(res.results[c]["out"])
    return out
